# version 56
# baseline (speedup 1.0000x reference)
import numpy as np
import ml_dtypes
from contextlib import ExitStack

import concourse.bass as bass
import concourse.mybir as mybir
from concourse.bass_utils import run_bass_kernel_spmd

F32 = mybir.dt.float32
BF16 = mybir.dt.bfloat16
AF = mybir.ActivationFunctionType
ALU = mybir.AluOpType
AX = mybir.AxisListType

D = 1024


def build_nc(S):
    T = S // 128
    nch = [(i + 8) // 8 for i in range(T)]
    nc = bass.Bass("TRN2", target_bir_lowering=False, debug=False)

    x_d = nc.dram_tensor("x", [S, D], F32, kind="ExternalInput")
    wq_d = nc.dram_tensor("wq", [16, D], F32, kind="ExternalInput")
    bq_d = nc.dram_tensor("bq", [16], F32, kind="ExternalInput")
    wk_d = nc.dram_tensor("wk", [16, D], F32, kind="ExternalInput")
    bk_d = nc.dram_tensor("bk", [16], F32, kind="ExternalInput")
    kern_d = nc.dram_tensor("kern", [D, D], F32, kind="ExternalInput")
    fb_d = nc.dram_tensor("fb", [D], F32, kind="ExternalInput")
    a_d = nc.dram_tensor("a", [D, 16], F32, kind="ExternalInput")
    b_d = nc.dram_tensor("b", [16, D], F32, kind="ExternalInput")
    out_d = nc.dram_tensor("out", [S, D], F32, kind="ExternalOutput")

    identbf_d = nc.inline_tensor(np.eye(128, dtype=ml_dtypes.bfloat16), name="identbf_c")
    identf_d = nc.inline_tensor(np.eye(128, dtype=np.float32), name="identf_c")
    maskneg_d = nc.inline_tensor(
        np.triu(np.full((128, 128), -1.0e6, np.float32), 1), name="maskneg_c")
    onesf_d = nc.inline_tensor(np.ones((1, 128), np.float32), name="onesf_c")
    onesbf_d = nc.inline_tensor(np.ones((1, 128), ml_dtypes.bfloat16), name="onesbf_c")

    with ExitStack() as st:
        e = st.enter_context
        V = e(nc.sbuf_tensor("V", [128, T * 1024], BF16))
        QTt = e(nc.sbuf_tensor("QTt", [16, 128], BF16))
        KT = e(nc.sbuf_tensor("KT", [16, S], BF16))
        Weff = e(nc.sbuf_tensor("Weff", [128, 8 * 1024], BF16))
        xbuf = [e(nc.sbuf_tensor(f"xbuf{j}", [128, 1024], F32)) for j in range(3)]
        x2buf = [e(nc.sbuf_tensor(f"x2buf{j}", [128, 1024], BF16)) for j in range(2)]
        P = e(nc.sbuf_tensor("P", [128, 1024], BF16))
        atn = e(nc.sbuf_tensor("atn", [128, 1024], BF16))
        PTsb = [e(nc.sbuf_tensor(f"PTsb{j}", [128, 1024], BF16)) for j in range(2)]
        xlnT = e(nc.sbuf_tensor("xlnT", [128, 1024], BF16))
        xln2 = e(nc.sbuf_tensor("xln2", [128, 1024], BF16))
        kst = [e(nc.sbuf_tensor(f"kst{j}", [128, 1024], F32)) for j in range(2)]
        xbuf.append(kst[1])
        aT = e(nc.sbuf_tensor("aT", [16, 1024], BF16))
        bsb = e(nc.sbuf_tensor("bsb", [16, 1024], BF16))
        biasrow = aT
        identbf = e(nc.sbuf_tensor("identbf", [128, 128], BF16))
        identf = e(nc.sbuf_tensor("identf", [128, 128], F32))
        maskneg = e(nc.sbuf_tensor("maskneg", [128, 128], F32))
        onesf = e(nc.sbuf_tensor("onesf", [1, 128], F32))
        onesbf = e(nc.sbuf_tensor("onesbf", [1, 128], BF16))
        wqT = e(nc.sbuf_tensor("wqT", [128, 128], BF16))
        wkT = e(nc.sbuf_tensor("wkT", [128, 128], BF16))
        s1a = e(nc.sbuf_tensor("s1a", [128, T], F32))
        sqa = e(nc.sbuf_tensor("sqa", [128, T], F32))
        ma = e(nc.sbuf_tensor("ma", [128, T], F32))
        va = e(nc.sbuf_tensor("va", [128, T], F32))
        ra = e(nc.sbuf_tensor("ra", [128, T], F32))
        na = e(nc.sbuf_tensor("na", [128, T], F32))
        s1b = e(nc.sbuf_tensor("s1b", [128, T], F32))
        sqb = e(nc.sbuf_tensor("sqb", [128, T], F32))
        mb = e(nc.sbuf_tensor("mb", [128, T], F32))
        vb = e(nc.sbuf_tensor("vb", [128, T], F32))
        rb = e(nc.sbuf_tensor("rb", [128, T], F32))
        nb = e(nc.sbuf_tensor("nb", [128, T], F32))
        ssum = e(nc.sbuf_tensor("ssum", [128, T], F32))
        rec = e(nc.sbuf_tensor("rec", [128, T], F32))
        satn = e(nc.sbuf_tensor("satn", [128, 1], F32))
        nM = e(nc.sbuf_tensor("nM", [128, 32], F32))
        fc = e(nc.sbuf_tensor("fc", [128, 32], F32))
        sc = e(nc.sbuf_tensor("sc", [128, 32], F32))
        mcol = e(nc.sbuf_tensor("mcol", [128, 1], F32))
        tmp1 = e(nc.sbuf_tensor("tmp1", [128, 1], F32))
        epscol = e(nc.sbuf_tensor("epscol", [128, 1], F32))
        amax8 = e(nc.sbuf_tensor("amax8", [128, 8], F32))
        sp = e(nc.sbuf_tensor("sp", [1, 8], F32))
        s4 = e(nc.sbuf_tensor("s4", [128, 2], F32))
        scl = e(nc.sbuf_tensor("scl", [128, 2], F32))
        th = e(nc.sbuf_tensor("th", [128, 16], F32))
        bqc = e(nc.sbuf_tensor("bqc", [16, 1], F32))
        bkc = e(nc.sbuf_tensor("bkc", [16, 1], F32))

        S_ps = e(nc.psum_tensor("S_ps", [128, 1024], F32))
        O_ps = e(nc.psum_tensor("O_ps", [128, 1024], F32))
        Y_ps = e(nc.psum_tensor("Y_ps", [128, 1024], F32))
        PT_ps = e(nc.psum_tensor("PT_ps", [128, 1024], BF16))
        TX_ps = e(nc.psum_tensor("TX_ps", [128, 1024], BF16))

        sem = {k: e(nc.semaphore(f"s_{k}")) for k in ["pe", "act", "dve", "pool", "sync"]}
        dsem = {k: e(nc.semaphore(f"d_{k}")) for k in ["x", "w", "o", "w2", "w4", "x2", "w3"]}
        block = e(nc.Block())

        sched = []
        cnt = {k: 0 for k in sem}
        dcnt = {k: 0 for k in dsem}
        ev = {}
        lastw = {k: {} for k in sem}

        def emit(eng, fn, name=None):
            semh = sem[eng]
            def thunk(E, fn=fn, semh=semh):
                fn(E).then_inc(semh, 1)
            sched.append((eng, thunk))
            cnt[eng] += 1
            if name is not None:
                ev[name] = (eng, semh, cnt[eng])

        def dma(eng, fn, dname, name=None):
            semh = dsem[dname]
            prev = dcnt[dname]
            if prev > 0:
                lw = lastw[eng]
                if lw.get(id(semh), -1) < prev:
                    lw[id(semh)] = prev
                    sched.append((eng, lambda E, semh=semh, prev=prev: E.wait_ge(semh, prev)))
            def thunk(E, fn=fn, semh=semh):
                fn(E).then_inc(semh, 16)
            sched.append((eng, thunk))
            dcnt[dname] += 16
            if name is not None:
                ev[name] = ("dma", semh, dcnt[dname])

        def wait(eng, name, force=False):
            src, semh, c = ev[name]
            if src == eng and not force:
                return
            lw = lastw[eng]
            if lw.get(id(semh), -1) >= c:
                return
            lw[id(semh)] = c
            sched.append((eng, lambda E, semh=semh, c=c: E.wait_ge(semh, c)))

        def selfsync(eng):
            semh, c = sem[eng], cnt[eng]
            lw = lastw[eng]
            if lw.get(id(semh), -1) >= c:
                return
            lw[id(semh)] = c
            sched.append((eng, lambda E, semh=semh, c=c: E.wait_ge(semh, c)))

        dma("sync", lambda E: E.dma_start(out=identf[:], in_=identf_d[:]), "w", "c_idf")
        dma("sync", lambda E: E.dma_start(out=identbf[:], in_=identbf_d[:]), "w2", "c_idb")
        dma("sync", lambda E: E.dma_start(out=maskneg[:], in_=maskneg_d[:]), "w2", "c_msk")
        dma("sync", lambda E: E.dma_start(out=onesf[:], in_=onesf_d[:]), "w3", "c_onf")
        dma("sync", lambda E: E.dma_start(out=onesbf[:], in_=onesbf_d[:]), "w3", "c_onb")
        dma("sync", lambda E: E.dma_start(out=bqc[:], in_=bq_d[:].rearrange("(e o) -> e o", o=1)), "x", "c_bq")
        dma("sync", lambda E: E.dma_start(out=bkc[:], in_=bk_d[:].rearrange("(e o) -> e o", o=1)), "x", "c_bk")

        emit("dve", lambda E: E.memset(epscol[:], 1e-5), "eps")
        wait("dve", "c_bq")
        emit("dve", lambda E: E.tensor_scalar(bqc[:], bqc[:], 0.25, None, ALU.mult), "bq4")

        dma("sync", lambda E: E.dma_start(out=xbuf[1][0:16, :], in_=wq_d[:]), "w", "ld_wq")
        dma("sync", lambda E: E.dma_start(out=xbuf[2][0:16, :], in_=wk_d[:]), "w", "ld_wk")
        wait("pe", "ld_wq"); wait("pe", "c_idf")
        for c in range(8):
            emit("pe", lambda E, c=c: E.matmul(
                S_ps[:, 16 * c:16 * (c + 1)], xbuf[1][0:16, 128 * c:128 * (c + 1)],
                identf[0:16, 0:16], start=True, stop=True, is_transpose=True),
                "t_wq" if c == 7 else None)
        wait("act", "t_wq")
        emit("act", lambda E: E.copy(wqT[:], S_ps[:, 0:128]), "e_wq")
        dma("act", lambda E: E.dma_start(
            out=xbuf[1][:, 0:128].rearrange("p (c r) -> p c r", c=8),
            in_=a_d[:].rearrange("(c p) r -> p c r", p=128)), "w2", "ld_a")
        wait("pe", "ld_wk")
        for c in range(8):
            emit("pe", lambda E, c=c: E.matmul(
                S_ps[:, 128 + 16 * c:128 + 16 * (c + 1)], xbuf[2][0:16, 128 * c:128 * (c + 1)],
                identf[0:16, 0:16], start=True, stop=True, is_transpose=True),
                "t_wk" if c == 7 else None)
        wait("act", "t_wk")
        emit("act", lambda E: E.copy(wkT[:], S_ps[:, 128:256]), "e_wk")
        dma("act", lambda E: E.dma_start(out=xbuf[2][0:16, :], in_=b_d[:]), "w2", "ld_b")
        wait("pe", "ld_a"); wait("pe", "e_wk")
        for c in range(8):
            emit("pe", lambda E, c=c: E.matmul(
                S_ps[0:16, 128 * c:128 * (c + 1)], xbuf[1][:, 16 * c:16 * (c + 1)],
                identf[:], start=True, stop=True, is_transpose=True),
                "t_a" if c == 7 else None)
        wait("act", "t_a")
        emit("act", lambda E: E.copy(aT[:], S_ps[0:16, 0:1024]), "e_aT")
        def kload(c, tag, waitev):
            buf = kst[c % 2]
            if tag == "kld":
                qs = [("sync", "w" if c % 2 == 0 else "x", 0, 64),
                      ("act", "w2" if c % 2 == 0 else "w3", 64, 128)]
            else:
                qs = [("sync", "w", 0, 64), ("act", "w2", 64, 128)]
            for qi, (qeng, qsem, p0, p1) in enumerate(qs):
                if waitev is not None:
                    wait(qeng, waitev)
                dma(qeng, lambda E, c=c, p0=p0, p1=p1, buf=buf: E.dma_start(
                    out=buf[p0:p1, :], in_=kern_d[128 * c + p0:128 * c + p1, :]),
                    qsem, f"{tag}{c}_{qi}")

        for c in range(8):
            kload(c, "kld", f"am{c - 2}" if c >= 2 else None)
            wait("dve", f"kld{c}_0"); wait("dve", f"kld{c}_1")
            emit("dve", lambda E, c=c: E.tensor_reduce(
                amax8[:, c:c + 1], kst[c % 2][:], AX.X, ALU.max, apply_absolute_value=True), f"am{c}")
        dma("sync", lambda E: E.dma_start(out=xbuf[0][:], in_=x_d[0:128, :]), "x", "xld0")
        wait("sync", "t_a"); wait("sync", "am7")
        dma("sync", lambda E: E.dma_start(out=xbuf[1][:], in_=x_d[128:256, :]), "w3", "xld1")
        selfsync("dve")
        emit("dve", lambda E: E.tensor_reduce(mcol[:], amax8[:], AX.X, ALU.max), "am_f")
        wait("pe", "am_f"); wait("pe", "e_aT")
        emit("pe", lambda E: E.matmul(S_ps[0:1, 0:128], mcol[:], identf[:],
                                      start=True, stop=True, is_transpose=True), "am_t")
        wait("dve", "am_t")
        emit("dve", lambda E: E.tensor_reduce(sp[0:1, 0:1], S_ps[0:1, 0:128], AX.X, ALU.max), "am_s")
        selfsync("dve")
        emit("dve", lambda E: E.tensor_scalar(sp[0:1, 1:2], sp[0:1, 0:1], 1.0 / 7.0, None, ALU.mult))
        emit("dve", lambda E: E.reciprocal(sp[0:1, 3:4], sp[0:1, 0:1]))
        selfsync("dve")
        emit("dve", lambda E: E.tensor_scalar(sp[0:1, 2:3], sp[0:1, 3:4], 7.0, None, ALU.mult), "am_rs")
        wait("pe", "am_rs"); wait("pe", "c_onf")
        emit("pe", lambda E: E.matmul(S_ps[:, 0:2], onesf[:], sp[0:1, 1:3],
                                      start=True, stop=True), "bc_mm")
        wait("dve", "bc_mm")
        emit("dve", lambda E: E.tensor_copy(scl[:], S_ps[:, 0:2]), "e_scl")
        selfsync("dve")
        for j in range(14):
            emit("dve", lambda E, j=j: E.tensor_scalar(
                th[:, j:j + 1], scl[:, 0:1], float(j) - 6.5, None, ALU.mult))
        emit("dve", lambda E: E.tensor_scalar(th[:, 14:15], scl[:, 0:1], -7.0, None, ALU.mult), "e_th")
        emit("dve", lambda E: E.tensor_scalar(s4[:, 0:1], scl[:, 0:1], -4.0, None, ALU.mult), None)
        emit("dve", lambda E: E.memset(s4[:, 1:2], 4.0), "e_s4")
        wait("dve", "ld_b")
        emit("dve", lambda E: E.tensor_scalar(bsb[:], xbuf[2][0:16, :], scl[0:16, 1:2], None, ALU.mult),
             "e_bsb")
        selfsync("dve")

        for c in range(8):
            kload(c, "k2ld", "am7" if c < 2 else f"cas{c - 2}")
            if c == 1:
                wait("sync", "e_bsb")
                dma("sync", lambda E: E.dma_start(out=xbuf[2][:], in_=x_d[256:384, :]), "x", "xld2")
            ps = [Y_ps, O_ps][c % 2]
            wait("pe", "e_bsb"); wait("pe", "e_aT")
            if c >= 2:
                wait("pe", f"wf{c - 2}")
            for h in range(2):
                emit("pe", lambda E, c=c, h=h, ps=ps: E.matmul(
                    ps[:, 512 * h:512 * (h + 1)], aT[:, 128 * c:128 * (c + 1)],
                    bsb[0:16, 512 * h:512 * (h + 1)], start=True, stop=True),
                    f"lora{c}" if h == 1 else None)
            buf = kst[c % 2]
            wait("dve", f"k2ld{c}_0"); wait("dve", f"k2ld{c}_1")
            wait("dve", f"lora{c}")
            emit("dve", lambda E, buf=buf: E.tensor_scalar(
                P[:], buf[:], th[:, 3:4], None, ALU.is_ge), None)
            selfsync("dve")
            for jj in (7, 11):
                emit("dve", lambda E, buf=buf, jj=jj: E.scalar_tensor_tensor(
                    P[:], buf[:], th[:, jj:jj + 1], P[:], ALU.is_ge, ALU.add), None)
                selfsync("dve")
            emit("dve", lambda E, buf=buf: E.scalar_tensor_tensor(
                buf[:], P[:], s4[:, 0:1], buf[:], ALU.mult, ALU.add), None)
            selfsync("dve")
            emit("dve", lambda E, ps=ps: E.scalar_tensor_tensor(
                ps[:], P[:], s4[:, 1:2], ps[:], ALU.mult, ALU.add), None)
            selfsync("dve")
            for jj in (0, 1):
                emit("dve", lambda E, buf=buf, jj=jj, ps=ps: E.scalar_tensor_tensor(
                    ps[:], buf[:], th[:, jj:jj + 1], ps[:], ALU.is_ge, ALU.add), None)
                selfsync("dve")
            emit("dve", lambda E, buf=buf, ps=ps: E.scalar_tensor_tensor(
                ps[:], buf[:], th[:, 2:3], ps[:], ALU.is_ge, ALU.add), f"cas{c}")
            selfsync("dve")
            emit("dve", lambda E, c=c, ps=ps: E.tensor_scalar(
                Weff[:, 1024 * c:1024 * (c + 1)], ps[:], scl[:, 0:1], th[:, 14:15],
                ALU.mult, ALU.add), f"wf{c}")

        wait("sync", "cas6")
        dma("sync", lambda E: E.dma_start(out=kst[0][0:1, :], in_=fb_d[:].rearrange("(o f) -> o f", o=1)),
            "w", "ld_fb")
        wait("dve", "ld_fb")
        wait("dve", "lora7")
        emit("dve", lambda E: E.tensor_copy(aT[0:1, :], kst[0][0:1, :]), "e_fb")

        def ln_stats(sfx, i, src, s1t, sqt, mt, vt, rt, nt, ready_ev, pre_sum=None):
            sg = "dve"
            if pre_sum is not None:
                wait(sg, pre_sum)
                emit(sg, lambda E, i=i: E.tensor_tensor(
                    s1t[:, i:i + 1], s1a[:, i:i + 1], satn[:], ALU.add), f"rs{sfx}_{i}")
                selfsync(sg)
                wait(sg, ready_ev, force=True)
                wait(sg, "e_fb")
            else:
                wait(sg, ready_ev, force=True)
                wait(sg, "e_fb")
                emit(sg, lambda E: E.reduce_sum(s1t[:, i:i + 1], src[:], AX.X), f"rs{sfx}_{i}")
                selfsync(sg)
            emit(sg, lambda E: E.scalar_tensor_tensor(
                kst[0][:], src[:], 0.0, src[:], ALU.bypass, ALU.mult,
                accum_out=sqt[:, i:i + 1]), f"sq{sfx}_{i}")
            selfsync(sg)
            emit(sg, lambda E: E.tensor_scalar(mt[:, i:i + 1], s1t[:, i:i + 1],
                                               1.0 / 1024.0, None, ALU.mult))
            selfsync(sg)
            emit(sg, lambda E: E.tensor_tensor(tmp1[:], mt[:, i:i + 1], mt[:, i:i + 1], ALU.mult))
            selfsync(sg)
            emit(sg, lambda E: E.scalar_tensor_tensor(
                vt[:, i:i + 1], sqt[:, i:i + 1], 1.0 / 1024.0, tmp1[:], ALU.mult, ALU.subtract),
                f"var{sfx}_{i}")
            wait("act", f"var{sfx}_{i}")
            emit("act", lambda E: E.activation(vt[:, i:i + 1], vt[:, i:i + 1], AF.Ln,
                                               bias=epscol[:]), None)
            selfsync("act")
            emit("act", lambda E: E.activation(rt[:, i:i + 1], vt[:, i:i + 1], AF.Exp,
                                               scale=-0.5), f"sd{sfx}_{i}")
            wait(sg, f"sd{sfx}_{i}")
            emit(sg, lambda E: E.scalar_tensor_tensor(
                nt[:, i:i + 1], mt[:, i:i + 1], -1.0, rt[:, i:i + 1], ALU.mult, ALU.mult),
                f"nmr{sfx}_{i}")

        def ln1(i):
            ln_stats("a", i, xbuf[i % 4], s1a, sqa, ma, va, ra, na, f"xld{i}")
            wait("pool", f"nmra_{i}", force=True)
            emit("pool", lambda E, i=i: E.tensor_scalar(
                V[:, 1024 * i:1024 * (i + 1)], xbuf[i % 4][:],
                ra[:, i:i + 1], na[:, i:i + 1], ALU.mult, ALU.add), f"ln1a_{i}")

        def ffn_tx(j):
            wait("pe", f"ln2a_{j}")
            if f"x1T_{j+1}" in ev:
                wait("pe", f"x1T_{j+1}")
            elif j >= 1:
                wait("pe", f"x2T_{j-1}")
            for c in range(8):
                emit("pe", lambda E, c=c: E.matmul(
                    TX_ps[:, 128 * c:128 * (c + 1)], xln2[:, 128 * c:128 * (c + 1)],
                    identbf[:], start=True, stop=True, is_transpose=True),
                    f"tx2_{j}" if c == 7 else None)
            wait("act", f"tx2_{j}")
            if f"projk_{j+1}" in ev:
                wait("act", f"projk_{j+1}")
            emit("act", lambda E: E.copy(xlnT[:], TX_ps[:]), f"x2T_{j}")

        def ffn_mm(j):
            wait("pe", f"x2T_{j}")
            wait("pe", "wf7")
            if j >= 1:
                wait("pe", f"t1_{j-1}")
            for c in range(8):
                for h in range(2):
                    emit("pe", lambda E, c=c, h=h: E.matmul(
                        Y_ps[:, 512 * h:512 * (h + 1)], xlnT[:, 128 * c:128 * (c + 1)],
                        Weff[:, 1024 * c + 512 * h:1024 * c + 512 * (h + 1)],
                        start=(c == 0), stop=False), None)
            wait("pe", "e_fb"); wait("pe", "c_onb")
            for h in range(2):
                emit("pe", lambda E, h=h: E.matmul(
                    Y_ps[:, 512 * h:512 * (h + 1)], onesbf[:],
                    biasrow[0:1, 512 * h:512 * (h + 1)], start=False, stop=True),
                    f"ffn_{j}" if h == 1 else None)

        def ffn_tail(j):
            wait("dve", f"ffn_{j}")
            wait("dve", f"x2_{j}")
            emit("dve", lambda E, j=j: E.tensor_tensor(
                xbuf[j % 4][:], Y_ps[:], x2buf[j % 2][:], ALU.add), f"t1_{j}")
            wait("pool", f"t1_{j}")
            dma("pool", lambda E, j=j: E.dma_start(
                out=out_d[128 * j:128 * (j + 1), :], in_=xbuf[j % 4][:]), "o", f"out_{j}")

        ln1(0)

        kser = 0
        ev_ept = [None]
        ev_avbuf = [None, None]
        def head_tx(j):
            wait("pe", f"ln1a_{j}")
            if j >= 2:
                wait("pe", f"x2T_{j-2}")
            for c in range(8):
                emit("pe", lambda E, j=j, c=c: E.matmul(
                    TX_ps[:, 128 * c:128 * (c + 1)],
                    V[:, 1024 * j + 128 * c:1024 * j + 128 * (c + 1)],
                    identbf[:], start=True, stop=True, is_transpose=True),
                    f"tx1_{j}" if c == 7 else None)
            wait("act", f"tx1_{j}")
            if j >= 2:
                wait("act", f"ffn_{j-2}")
            emit("act", lambda E: E.copy(xlnT[:], TX_ps[:]), f"x1T_{j}")

        def head_proj(j):
            wait("pe", f"x1T_{j}")
            if j >= 1:
                wait("pe", f"exp_{j-1}_{nch[j-1]-1}")
            for c in range(8):
                emit("pe", lambda E, c=c: E.matmul(
                    S_ps[0:16, 0:128], wqT[:, 16 * c:16 * (c + 1)],
                    xlnT[:, 128 * c:128 * (c + 1)], start=(c == 0), stop=(c == 7)),
                    f"projq_{j}" if c == 7 else None)
            for c in range(8):
                emit("pe", lambda E, c=c: E.matmul(
                    S_ps[0:16, 128:256], wkT[:, 16 * c:16 * (c + 1)],
                    xlnT[:, 128 * c:128 * (c + 1)], start=(c == 0), stop=(c == 7)),
                    f"projk_{j}" if c == 7 else None)
            wait("act", f"projq_{j}")
            wait("act", f"projk_{j}")
            if j >= 1:
                wait("act", f"S_{j-1}_{nch[j-1]-1}")
            emit("act", lambda E: E.activation(
                QTt[:], S_ps[0:16, 0:128], AF.Identity,
                bias=bqc[:], scale=0.25), f"eq_{j}")
            emit("act", lambda E, j=j: E.activation(
                KT[:, 128 * j:128 * (j + 1)], S_ps[0:16, 128:256], AF.Identity,
                bias=bkc[:]), f"ek_{j}")

        kb = [0]
        for i in range(T):
            kb.append(kb[-1] + nch[i])
        cpt = []
        for i in range(T):
            row = []
            for c in range(nch[i]):
                k0 = 1024 * c
                ncols = min(1024, 128 * (i + 1) - k0)
                row.append(dict(k0=k0, ncols=ncols, ktiles=ncols // 128,
                                km=(kb[i] + c) % 32, kmp=(kb[i] + c - 1) % 32,
                                diag=(c == nch[i] - 1)))
            cpt.append(row)

        def em_S(i, c):
            p = cpt[i][c]
            wait("pe", f"eq_{i}"); wait("pe", f"ek_{i}")
            if c >= 1:
                wait("pe", f"exp_{i}_{c-1}")
            nS = (p["ncols"] + 511) // 512
            for h in range(nS):
                w_ = min(512, p["ncols"] - 512 * h)
                emit("pe", lambda E, k0=p["k0"], h=h, w_=w_: E.matmul(
                    S_ps[:, 512 * h:512 * h + w_], QTt[:],
                    KT[:, k0 + 512 * h:k0 + 512 * h + w_], start=True, stop=True),
                    f"S_{i}_{c}" if h == nS - 1 else f"S_{i}_{c}_h0")

        def em_front(i, c):
            p = cpt[i][c]
            n = p["ncols"]
            km, kmp = p["km"], p["kmp"]
            pd = i % 8

            def red(lo, hi, ev_=None):
                emit("dve", lambda E, lo=lo, hi=hi: E.tensor_reduce(
                    mcol[:], S_ps[:, lo:hi], AX.X, ALU.max, negate=True), ev_)
                selfsync("dve")

            def fold(first, ev_=None):
                if first and c == 0:
                    emit("dve", lambda E, km=km: E.tensor_copy(nM[:, km:km + 1], mcol[:]), ev_)
                elif first:
                    emit("dve", lambda E, km=km, kmp=kmp: E.tensor_scalar(
                        nM[:, km:km + 1], mcol[:], nM[:, kmp:kmp + 1], None, ALU.min), ev_)
                else:
                    emit("dve", lambda E, km=km: E.tensor_scalar(
                        nM[:, km:km + 1], mcol[:], nM[:, km:km + 1], None, ALU.min), ev_)
                selfsync("dve")

            def msk():
                wait("dve", "c_msk")
                emit("dve", lambda E, pd=pd: E.tensor_tensor(
                    S_ps[:, 128 * pd:128 * (pd + 1)],
                    S_ps[:, 128 * pd:128 * (pd + 1)], maskneg[:], ALU.add),
                    f"msk_{i}_{c}")
                selfsync("dve")

            nme = f"nM_{i}_{c}"
            wait("dve", f"S_{i}_{c}")
            if p["diag"]:
                msk()
            red(0, n)
            fold(True, nme)
            if c >= 1:
                wait("act", f"nM_{i}_{c}")
                emit("act", lambda E, km=km, kmp=kmp: E.activation(
                    fc[:, km:km + 1], nM[:, kmp:kmp + 1], AF.Exp,
                    bias=nM[:, km:km + 1], scale=-1.0), f"fc_{i}_{c}")
            wait("act", f"nM_{i}_{c}")
            if c == 0 and i >= 1:
                wait("act", f"T_{i-1}_{nch[i-1]-1}")
            if c == 0 and i == 0:
                wait("act", "cas7")
            if c >= 1:
                wait("act", f"T_{i}_{c-1}")
            emit("act", lambda E, n=p["ncols"], km=p["km"]: E.activation(
                P[:, 0:n], S_ps[:, 0:n], AF.Exp,
                bias=nM[:, km:km + 1], accum_out=sc[:, km:km + 1]), f"exp_{i}_{c}")

        def em_T(i, k):
            wait("pe", f"exp_{i}_{k}")
            if ev_ept[0] is not None:
                wait("pe", ev_ept[0])
            for t in range(cpt[i][k]["ktiles"]):
                emit("pe", lambda E, t=t: E.matmul(
                    PT_ps[:, 128 * t:128 * (t + 1)], P[:, 128 * t:128 * (t + 1)],
                    identbf[:], start=True, stop=True, is_transpose=True),
                    f"T_{i}_{k}" if t == cpt[i][k]["ktiles"] - 1 else None)

        def em_ePT(i, k):
            wait("act", f"T_{i}_{k}")
            buf = k % 2
            n = cpt[i][k]["ncols"]
            if ev_avbuf[buf] is not None:
                wait("act", ev_avbuf[buf])
            emit("act", lambda E, n=n, buf=buf: E.copy(
                PTsb[buf][:, 0:n], PT_ps[:, 0:n]), f"ePT_{i}_{k}")
            ev_ept[0] = f"ePT_{i}_{k}"

        def em_rs(i, k):
            if k >= 1:
                wait("dve", f"fc_{i}_{k}")
                wait("dve", f"AV_{i}_{k-1}")
                emit("dve", lambda E, km=cpt[i][k]["km"]: E.tensor_scalar(
                    O_ps[:, 0:512], O_ps[:, 0:512], fc[:, km:km + 1], None, ALU.mult),
                    f"rsc0_{i}_{k}")
                emit("dve", lambda E, km=cpt[i][k]["km"]: E.tensor_scalar(
                    O_ps[:, 512:1024], O_ps[:, 512:1024], fc[:, km:km + 1], None, ALU.mult),
                    f"rsc_{i}_{k}")
            wait("dve", f"exp_{i}_{k}")
            if k >= 1:
                selfsync("dve")
            if k == 0:
                emit("dve", lambda E, i=i, km=cpt[i][k]["km"]: E.tensor_copy(
                    ssum[:, i:i + 1], sc[:, km:km + 1]), f"ssum_{i}_{k}")
            else:
                emit("dve", lambda E, i=i, km=cpt[i][k]["km"]: E.scalar_tensor_tensor(
                    ssum[:, i:i + 1], ssum[:, i:i + 1], fc[:, km:km + 1],
                    sc[:, km:km + 1], ALU.mult, ALU.add), f"ssum_{i}_{k}")

        def em_AV(i, k):
            p = cpt[i][k]
            if k == 0 and i == 0:
                wait("pe", "wf7")
            if k == 0 and i >= 1:
                wait("pe", f"tat_{i-1}")
            buf = k % 2
            for h in range(2):
                if k >= 1:
                    wait("pe", f"rsc0_{i}_{k}" if h == 0 else f"rsc_{i}_{k}")
                for t in range(p["ktiles"]):
                    wait("pe", f"ePT_{i}_{k}")
                    jj = 8 * k + t
                    last = (t == p["ktiles"] - 1 and h == 1)
                    emit("pe", lambda E, jj=jj, t=t, h=h, buf=buf,
                         st=(k == 0 and t == 0), sp=(p["diag"] and last): E.matmul(
                        O_ps[:, 512 * h:512 * (h + 1)], PTsb[buf][:, 128 * t:128 * (t + 1)],
                        V[:, 1024 * jj + 512 * h:1024 * jj + 512 * (h + 1)],
                        start=st, stop=sp, skip_group_check=True),
                        f"AV_{i}_{k}" if last else None)
            ev_avbuf[buf] = f"AV_{i}_{k}"

        head_tx(0)
        head_proj(0)
        em_S(0, 0)
        em_front(0, 0)
        for i in range(T):
            L = nch[i] - 1
            j = i + 2
            if 3 <= j < T:
                if j == 3:
                    wait("sync", "cas7")
                if f"out_{j-4}" in ev:
                    wait("sync", f"out_{j-4}")
                dma("sync", lambda E, j=j: E.dma_start(
                    out=xbuf[j % 4][:], in_=x_d[128 * j:128 * (j + 1), :]), "x", f"xld{j}")
            if i + 1 < T:
                ln1(i + 1)

            for c in range(1, nch[i]):
                em_S(i, c)
                if c == 1 and i >= 1:
                    ffn_tx(i - 1)
                em_T(i, c - 1)
                if c == 1 and i >= 1:
                    ffn_mm(i - 1)
                if c >= 2:
                    em_AV(i, c - 2)
                em_ePT(i, c - 1)
                em_front(i, c)
                em_rs(i, c - 1)
            em_T(i, L)
            if L >= 1:
                em_AV(i, L - 1)
            em_ePT(i, L)
            em_rs(i, L)
            selfsync("dve")
            emit("dve", lambda E, i=i: E.reciprocal(rec[:, i:i + 1], ssum[:, i:i + 1]), f"rec_{i}")
            if L == 0 and i >= 1:
                ffn_tx(i - 1)
                ffn_mm(i - 1)
            if i + 1 < T:
                head_tx(i + 1)
            em_AV(i, L)
            if i + 1 < T:
                head_proj(i + 1)
            ev[f"AV_{i}"] = ev[f"AV_{i}_{L}"]

            wait("act", f"rec_{i}")
            wait("act", f"AV_{i}")
            if i >= 1:
                wait("act", f"x2_{i-1}")
            emit("act", lambda E, i=i: E.activation(
                atn[:], O_ps[:], AF.Identity, scale=rec[:, i:i + 1],
                accum_out=satn[:]), f"tat_{i}")
            wait("dve", f"tat_{i}")
            if i >= 2:
                wait("dve", f"t1_{i-2}")
            emit("dve", lambda E, i=i: E.tensor_tensor(
                x2buf[i % 2][:], atn[:], xbuf[i % 4][:], ALU.add), f"x2_{i}")

            if i + 1 < T:
                em_S(i + 1, 0)
                em_front(i + 1, 0)
            if i >= 1:
                ffn_tail(i - 1)

            ln_stats("b", i, x2buf[i % 2], s1b, sqb, mb, vb, rb, nb, f"x2_{i}",
                     pre_sum=f"tat_{i}")

            wait("pool", f"nmrb_{i}", force=True)
            if i >= 1:
                wait("pool", f"tx2_{i-1}")
            emit("pool", lambda E, i=i: E.tensor_scalar(
                xln2[:], x2buf[i % 2][:], rb[:, i:i + 1], nb[:, i:i + 1],
                ALU.mult, ALU.add), f"ln2a_{i}")


        ffn_tx(T - 1)
        ffn_mm(T - 1)
        ffn_tail(T - 1)
        c_o = dcnt["o"]
        sched.append(("pool", lambda E, c_o=c_o: E.wait_ge(dsem["o"], c_o)))

        def runner(engname):
            def body(E):
                for en, thunk in sched:
                    if en == engname:
                        thunk(E)
            return body

        block.sync(runner("sync"))
        block.tensor(runner("pe"))
        block.vector(runner("dve"))
        block.scalar(runner("act"))
        block.gpsimd(runner("pool"))

    return nc


def kernel(L0_attn_x, L0_attn_w_q, L0_attn_b_q, L0_attn_w_k, L0_attn_b_k,
           L0_ffn_kernel, L0_ffn_bias, L0_ffn_a, L0_ffn_b):
    x = np.ascontiguousarray(np.asarray(L0_attn_x, dtype=np.float32))
    wq = np.ascontiguousarray(np.asarray(L0_attn_w_q, dtype=np.float32))
    bq = np.ascontiguousarray(np.asarray(L0_attn_b_q, dtype=np.float32))
    wk = np.ascontiguousarray(np.asarray(L0_attn_w_k, dtype=np.float32))
    bk = np.ascontiguousarray(np.asarray(L0_attn_b_k, dtype=np.float32))
    kern = np.ascontiguousarray(np.asarray(L0_ffn_kernel, dtype=np.float32))
    fb = np.ascontiguousarray(np.asarray(L0_ffn_bias, dtype=np.float32))
    a = np.ascontiguousarray(np.asarray(L0_ffn_a, dtype=np.float32))
    b = np.ascontiguousarray(np.asarray(L0_ffn_b, dtype=np.float32))
    S = x.shape[1]
    nc = build_nc(S)
    in_maps = [{"x": np.ascontiguousarray(x[c]), "wq": wq, "bq": bq, "wk": wk, "bk": bk,
                "kern": kern, "fb": fb, "a": a, "b": b} for c in range(x.shape[0])]
    res = run_bass_kernel_spmd(nc, in_maps, core_ids=list(range(x.shape[0])))
    return np.stack([r["out"] for r in res.results], axis=0)


# version 57
# speedup vs baseline: 1.0239x; 1.0239x over previous
import numpy as np
import ml_dtypes
from contextlib import ExitStack

import concourse.bass as bass
import concourse.mybir as mybir
from concourse.bass_utils import run_bass_kernel_spmd

F32 = mybir.dt.float32
BF16 = mybir.dt.bfloat16
AF = mybir.ActivationFunctionType
ALU = mybir.AluOpType
AX = mybir.AxisListType

D = 1024


def build_nc(S):
    T = S // 128
    nch = [(i + 8) // 8 for i in range(T)]
    nc = bass.Bass("TRN2", target_bir_lowering=False, debug=False)

    x_d = nc.dram_tensor("x", [S, D], F32, kind="ExternalInput")
    wq_d = nc.dram_tensor("wq", [16, D], F32, kind="ExternalInput")
    bq_d = nc.dram_tensor("bq", [16], F32, kind="ExternalInput")
    wk_d = nc.dram_tensor("wk", [16, D], F32, kind="ExternalInput")
    bk_d = nc.dram_tensor("bk", [16], F32, kind="ExternalInput")
    kern_d = nc.dram_tensor("kern", [D, D], F32, kind="ExternalInput")
    fb_d = nc.dram_tensor("fb", [D], F32, kind="ExternalInput")
    a_d = nc.dram_tensor("a", [D, 16], F32, kind="ExternalInput")
    b_d = nc.dram_tensor("b", [16, D], F32, kind="ExternalInput")
    out_d = nc.dram_tensor("out", [S, D], F32, kind="ExternalOutput")

    identbf_d = nc.inline_tensor(np.eye(128, dtype=ml_dtypes.bfloat16), name="identbf_c")
    identf_d = nc.inline_tensor(np.eye(128, dtype=np.float32), name="identf_c")
    maskneg_d = nc.inline_tensor(
        np.triu(np.full((128, 128), -1.0e6, np.float32), 1), name="maskneg_c")
    onesf_d = nc.inline_tensor(np.ones((1, 128), np.float32), name="onesf_c")
    onesbf_d = nc.inline_tensor(np.ones((1, 128), ml_dtypes.bfloat16), name="onesbf_c")

    with ExitStack() as st:
        e = st.enter_context
        V = e(nc.sbuf_tensor("V", [128, T * 1024], BF16))
        QTt = e(nc.sbuf_tensor("QTt", [16, 128], BF16))
        KT = e(nc.sbuf_tensor("KT", [16, S], BF16))
        Weff = e(nc.sbuf_tensor("Weff", [128, 8 * 1024], BF16))
        xbuf = [e(nc.sbuf_tensor(f"xbuf{j}", [128, 1024], F32)) for j in range(3)]
        x2buf = [e(nc.sbuf_tensor(f"x2buf{j}", [128, 1024], BF16)) for j in range(2)]
        P = e(nc.sbuf_tensor("P", [128, 1024], BF16))
        atn = e(nc.sbuf_tensor("atn", [128, 1024], BF16))
        PTsb = [e(nc.sbuf_tensor(f"PTsb{j}", [128, 1024], BF16)) for j in range(2)]
        xlnT = e(nc.sbuf_tensor("xlnT", [128, 1024], BF16))
        xln2 = e(nc.sbuf_tensor("xln2", [128, 1024], BF16))
        kst = [e(nc.sbuf_tensor(f"kst{j}", [128, 1024], F32)) for j in range(2)]
        xbuf.append(kst[1])
        aT = e(nc.sbuf_tensor("aT", [16, 1024], BF16))
        bsb = e(nc.sbuf_tensor("bsb", [16, 1024], BF16))
        biasrow = aT
        identbf = e(nc.sbuf_tensor("identbf", [128, 128], BF16))
        identf = e(nc.sbuf_tensor("identf", [128, 128], F32))
        maskneg = e(nc.sbuf_tensor("maskneg", [128, 128], F32))
        onesf = e(nc.sbuf_tensor("onesf", [1, 128], F32))
        onesbf = e(nc.sbuf_tensor("onesbf", [1, 128], BF16))
        wqT = e(nc.sbuf_tensor("wqT", [128, 128], BF16))
        wkT = e(nc.sbuf_tensor("wkT", [128, 128], BF16))
        s1a = e(nc.sbuf_tensor("s1a", [128, T], F32))
        sqa = e(nc.sbuf_tensor("sqa", [128, T], F32))
        ma = e(nc.sbuf_tensor("ma", [128, T], F32))
        va = e(nc.sbuf_tensor("va", [128, T], F32))
        ra = e(nc.sbuf_tensor("ra", [128, T], F32))
        na = e(nc.sbuf_tensor("na", [128, T], F32))
        s1b = e(nc.sbuf_tensor("s1b", [128, T], F32))
        sqb = e(nc.sbuf_tensor("sqb", [128, T], F32))
        mb = e(nc.sbuf_tensor("mb", [128, T], F32))
        vb = e(nc.sbuf_tensor("vb", [128, T], F32))
        rb = e(nc.sbuf_tensor("rb", [128, T], F32))
        nb = e(nc.sbuf_tensor("nb", [128, T], F32))
        ssum = e(nc.sbuf_tensor("ssum", [128, T], F32))
        rec = e(nc.sbuf_tensor("rec", [128, T], F32))
        satn = e(nc.sbuf_tensor("satn", [128, 1], F32))
        nM = e(nc.sbuf_tensor("nM", [128, 32], F32))
        fc = e(nc.sbuf_tensor("fc", [128, 32], F32))
        sc = e(nc.sbuf_tensor("sc", [128, 32], F32))
        mcol = e(nc.sbuf_tensor("mcol", [128, 1], F32))
        tmp1 = e(nc.sbuf_tensor("tmp1", [128, 1], F32))
        epscol = e(nc.sbuf_tensor("epscol", [128, 1], F32))
        amax8 = e(nc.sbuf_tensor("amax8", [128, 8], F32))
        sp = e(nc.sbuf_tensor("sp", [1, 8], F32))
        s4 = e(nc.sbuf_tensor("s4", [128, 2], F32))
        scl = e(nc.sbuf_tensor("scl", [128, 2], F32))
        th = e(nc.sbuf_tensor("th", [128, 16], F32))
        bqc = e(nc.sbuf_tensor("bqc", [16, 1], F32))
        bkc = e(nc.sbuf_tensor("bkc", [16, 1], F32))

        S_ps = e(nc.psum_tensor("S_ps", [128, 1024], F32))
        O_ps = e(nc.psum_tensor("O_ps", [128, 1024], F32))
        Y_ps = e(nc.psum_tensor("Y_ps", [128, 1024], F32))
        PT_ps = e(nc.psum_tensor("PT_ps", [128, 1024], BF16))
        TX_ps = e(nc.psum_tensor("TX_ps", [128, 1024], BF16))

        sem = {k: e(nc.semaphore(f"s_{k}")) for k in ["pe", "act", "dve", "pool", "sync"]}
        dsem = {k: e(nc.semaphore(f"d_{k}")) for k in ["x", "w", "o", "w2", "w4", "x2", "w3"]}
        block = e(nc.Block())

        sched = []
        cnt = {k: 0 for k in sem}
        dcnt = {k: 0 for k in dsem}
        ev = {}
        lastw = {k: {} for k in sem}

        def emit(eng, fn, name=None):
            semh = sem[eng]
            def thunk(E, fn=fn, semh=semh):
                fn(E).then_inc(semh, 1)
            sched.append((eng, thunk))
            cnt[eng] += 1
            if name is not None:
                ev[name] = (eng, semh, cnt[eng])

        def dma(eng, fn, dname, name=None):
            semh = dsem[dname]
            prev = dcnt[dname]
            if prev > 0:
                lw = lastw[eng]
                if lw.get(id(semh), -1) < prev:
                    lw[id(semh)] = prev
                    sched.append((eng, lambda E, semh=semh, prev=prev: E.wait_ge(semh, prev)))
            def thunk(E, fn=fn, semh=semh):
                fn(E).then_inc(semh, 16)
            sched.append((eng, thunk))
            dcnt[dname] += 16
            if name is not None:
                ev[name] = ("dma", semh, dcnt[dname])

        def wait(eng, name, force=False):
            src, semh, c = ev[name]
            if src == eng and not force:
                return
            lw = lastw[eng]
            if lw.get(id(semh), -1) >= c:
                return
            lw[id(semh)] = c
            sched.append((eng, lambda E, semh=semh, c=c: E.wait_ge(semh, c)))

        def selfsync(eng):
            semh, c = sem[eng], cnt[eng]
            lw = lastw[eng]
            if lw.get(id(semh), -1) >= c:
                return
            lw[id(semh)] = c
            sched.append((eng, lambda E, semh=semh, c=c: E.wait_ge(semh, c)))

        dma("sync", lambda E: E.dma_start(out=identf[:], in_=identf_d[:]), "w", "c_idf")
        dma("sync", lambda E: E.dma_start(out=identbf[:], in_=identbf_d[:]), "w2", "c_idb")
        dma("sync", lambda E: E.dma_start(out=maskneg[:], in_=maskneg_d[:]), "w2", "c_msk")
        dma("sync", lambda E: E.dma_start(out=onesf[:], in_=onesf_d[:]), "w3", "c_onf")
        dma("sync", lambda E: E.dma_start(out=onesbf[:], in_=onesbf_d[:]), "w3", "c_onb")
        dma("sync", lambda E: E.dma_start(out=bqc[:], in_=bq_d[:].rearrange("(e o) -> e o", o=1)), "x", "c_bq")
        dma("sync", lambda E: E.dma_start(out=bkc[:], in_=bk_d[:].rearrange("(e o) -> e o", o=1)), "x", "c_bk")

        emit("dve", lambda E: E.memset(epscol[:], 1e-5), "eps")
        wait("dve", "c_bq")
        emit("dve", lambda E: E.tensor_scalar(bqc[:], bqc[:], 0.25, None, ALU.mult), "bq4")

        dma("sync", lambda E: E.dma_start(out=xbuf[1][0:16, :], in_=wq_d[:]), "w", "ld_wq")
        dma("sync", lambda E: E.dma_start(out=xbuf[2][0:16, :], in_=wk_d[:]), "w", "ld_wk")
        wait("pe", "ld_wq"); wait("pe", "c_idf")
        for c in range(8):
            emit("pe", lambda E, c=c: E.matmul(
                S_ps[:, 16 * c:16 * (c + 1)], xbuf[1][0:16, 128 * c:128 * (c + 1)],
                identf[0:16, 0:16], start=True, stop=True, is_transpose=True),
                "t_wq" if c == 7 else None)
        wait("act", "t_wq")
        emit("act", lambda E: E.copy(wqT[:], S_ps[:, 0:128]), "e_wq")
        dma("act", lambda E: E.dma_start(
            out=xbuf[1][:, 0:128].rearrange("p (c r) -> p c r", c=8),
            in_=a_d[:].rearrange("(c p) r -> p c r", p=128)), "w2", "ld_a")
        wait("pe", "ld_wk")
        for c in range(8):
            emit("pe", lambda E, c=c: E.matmul(
                S_ps[:, 128 + 16 * c:128 + 16 * (c + 1)], xbuf[2][0:16, 128 * c:128 * (c + 1)],
                identf[0:16, 0:16], start=True, stop=True, is_transpose=True),
                "t_wk" if c == 7 else None)
        wait("act", "t_wk")
        emit("act", lambda E: E.copy(wkT[:], S_ps[:, 128:256]), "e_wk")
        dma("act", lambda E: E.dma_start(out=xbuf[2][0:16, :], in_=b_d[:]), "w2", "ld_b")
        wait("pe", "ld_a"); wait("pe", "e_wk")
        for c in range(8):
            emit("pe", lambda E, c=c: E.matmul(
                S_ps[0:16, 128 * c:128 * (c + 1)], xbuf[1][:, 16 * c:16 * (c + 1)],
                identf[:], start=True, stop=True, is_transpose=True),
                "t_a" if c == 7 else None)
        wait("act", "t_a")
        emit("act", lambda E: E.copy(aT[:], S_ps[0:16, 0:1024]), "e_aT")
        def kload(c, tag, waitev):
            buf = kst[c % 2]
            if tag == "kld":
                qs = [("sync", "w" if c % 2 == 0 else "x", 0, 64),
                      ("act", "w2" if c % 2 == 0 else "w3", 64, 128)]
            else:
                qs = [("sync", "w", 0, 64), ("act", "w2", 64, 128)]
            for qi, (qeng, qsem, p0, p1) in enumerate(qs):
                if waitev is not None:
                    wait(qeng, waitev)
                dma(qeng, lambda E, c=c, p0=p0, p1=p1, buf=buf: E.dma_start(
                    out=buf[p0:p1, :], in_=kern_d[128 * c + p0:128 * c + p1, :]),
                    qsem, f"{tag}{c}_{qi}")

        for c in range(8):
            kload(c, "kld", f"am{c - 2}" if c >= 2 else None)
            wait("dve", f"kld{c}_0"); wait("dve", f"kld{c}_1")
            emit("dve", lambda E, c=c: E.tensor_reduce(
                amax8[:, c:c + 1], kst[c % 2][:], AX.X, ALU.max, apply_absolute_value=True), f"am{c}")
        dma("sync", lambda E: E.dma_start(out=xbuf[0][:], in_=x_d[0:128, :]), "x", "xld0")
        wait("sync", "t_a"); wait("sync", "am7")
        dma("sync", lambda E: E.dma_start(out=xbuf[1][:], in_=x_d[128:256, :]), "w3", "xld1")
        selfsync("dve")
        emit("dve", lambda E: E.tensor_reduce(mcol[:], amax8[:], AX.X, ALU.max), "am_f")
        wait("pe", "am_f"); wait("pe", "e_aT")
        emit("pe", lambda E: E.matmul(S_ps[0:1, 0:128], mcol[:], identf[:],
                                      start=True, stop=True, is_transpose=True), "am_t")
        wait("dve", "am_t")
        emit("dve", lambda E: E.tensor_reduce(sp[0:1, 0:1], S_ps[0:1, 0:128], AX.X, ALU.max), "am_s")
        selfsync("dve")
        emit("dve", lambda E: E.tensor_scalar(sp[0:1, 1:2], sp[0:1, 0:1], 1.0 / 7.0, None, ALU.mult))
        emit("dve", lambda E: E.reciprocal(sp[0:1, 3:4], sp[0:1, 0:1]))
        selfsync("dve")
        emit("dve", lambda E: E.tensor_scalar(sp[0:1, 2:3], sp[0:1, 3:4], 7.0, None, ALU.mult), "am_rs")
        wait("pe", "am_rs"); wait("pe", "c_onf")
        emit("pe", lambda E: E.matmul(S_ps[:, 0:2], onesf[:], sp[0:1, 1:3],
                                      start=True, stop=True), "bc_mm")
        wait("dve", "bc_mm")
        emit("dve", lambda E: E.tensor_copy(scl[:], S_ps[:, 0:2]), "e_scl")
        selfsync("dve")
        for j in range(14):
            emit("dve", lambda E, j=j: E.tensor_scalar(
                th[:, j:j + 1], scl[:, 0:1], float(j) - 6.5, None, ALU.mult))
        emit("dve", lambda E: E.tensor_scalar(th[:, 14:15], scl[:, 0:1], -7.0, None, ALU.mult), "e_th")
        emit("dve", lambda E: E.tensor_scalar(s4[:, 0:1], scl[:, 0:1], -4.0, None, ALU.mult), None)
        emit("dve", lambda E: E.memset(s4[:, 1:2], 4.0), "e_s4")
        wait("dve", "ld_b")
        emit("dve", lambda E: E.tensor_scalar(bsb[:], xbuf[2][0:16, :], scl[0:16, 1:2], None, ALU.mult),
             "e_bsb")
        selfsync("dve")

        for c in range(8):
            kload(c, "k2ld", "am7" if c < 2 else f"cas{c - 2}")
            if c == 1:
                wait("sync", "e_bsb")
                dma("sync", lambda E: E.dma_start(out=xbuf[2][:], in_=x_d[256:384, :]), "x", "xld2")
            ps = [Y_ps, O_ps][c % 2]
            wait("pe", "e_bsb"); wait("pe", "e_aT")
            if c >= 2:
                wait("pe", f"wf{c - 2}")
            for h in range(2):
                emit("pe", lambda E, c=c, h=h, ps=ps: E.matmul(
                    ps[:, 512 * h:512 * (h + 1)], aT[:, 128 * c:128 * (c + 1)],
                    bsb[0:16, 512 * h:512 * (h + 1)], start=True, stop=True),
                    f"lora{c}" if h == 1 else None)
            buf = kst[c % 2]
            wait("dve", f"k2ld{c}_0"); wait("dve", f"k2ld{c}_1")
            wait("dve", f"lora{c}")
            emit("dve", lambda E, buf=buf: E.tensor_scalar(
                P[:], buf[:], th[:, 3:4], None, ALU.is_ge), None)
            selfsync("dve")
            for jj in (7, 11):
                emit("dve", lambda E, buf=buf, jj=jj: E.scalar_tensor_tensor(
                    P[:], buf[:], th[:, jj:jj + 1], P[:], ALU.is_ge, ALU.add), None)
                selfsync("dve")
            emit("dve", lambda E, buf=buf: E.scalar_tensor_tensor(
                buf[:], P[:], s4[:, 0:1], buf[:], ALU.mult, ALU.add), None)
            selfsync("dve")
            emit("dve", lambda E, ps=ps: E.scalar_tensor_tensor(
                ps[:], P[:], s4[:, 1:2], ps[:], ALU.mult, ALU.add), None)
            selfsync("dve")
            for jj in (0, 1):
                emit("dve", lambda E, buf=buf, jj=jj, ps=ps: E.scalar_tensor_tensor(
                    ps[:], buf[:], th[:, jj:jj + 1], ps[:], ALU.is_ge, ALU.add), None)
                selfsync("dve")
            emit("dve", lambda E, buf=buf, ps=ps: E.scalar_tensor_tensor(
                ps[:], buf[:], th[:, 2:3], ps[:], ALU.is_ge, ALU.add), f"cas{c}")
            selfsync("dve")
            emit("dve", lambda E, c=c, ps=ps: E.tensor_scalar(
                Weff[:, 1024 * c:1024 * (c + 1)], ps[:], scl[:, 0:1], th[:, 14:15],
                ALU.mult, ALU.add), f"wf{c}")

        wait("sync", "cas6")
        dma("sync", lambda E: E.dma_start(out=kst[0][0:1, :], in_=fb_d[:].rearrange("(o f) -> o f", o=1)),
            "w", "ld_fb")
        wait("dve", "ld_fb")
        wait("dve", "lora7")
        emit("dve", lambda E: E.tensor_copy(aT[0:1, :], kst[0][0:1, :]), "e_fb")

        def ln_stats(sfx, i, src, s1t, sqt, mt, vt, rt, nt, ready_ev, pre_sum=None):
            sg = "dve"
            if pre_sum is not None:
                wait(sg, pre_sum)
                emit(sg, lambda E, i=i: E.tensor_tensor(
                    s1t[:, i:i + 1], s1a[:, i:i + 1], satn[:], ALU.add), f"rs{sfx}_{i}")
                selfsync(sg)
                wait(sg, ready_ev, force=True)
                wait(sg, "e_fb")
            else:
                wait(sg, ready_ev, force=True)
                wait(sg, "e_fb")
                emit(sg, lambda E: E.reduce_sum(s1t[:, i:i + 1], src[:], AX.X), f"rs{sfx}_{i}")
                selfsync(sg)
            emit(sg, lambda E: E.scalar_tensor_tensor(
                kst[0][:], src[:], 0.0, src[:], ALU.bypass, ALU.mult,
                accum_out=sqt[:, i:i + 1]), f"sq{sfx}_{i}")
            selfsync(sg)
            emit(sg, lambda E: E.tensor_scalar(mt[:, i:i + 1], s1t[:, i:i + 1],
                                               1.0 / 1024.0, None, ALU.mult))
            selfsync(sg)
            emit(sg, lambda E: E.tensor_tensor(tmp1[:], mt[:, i:i + 1], mt[:, i:i + 1], ALU.mult))
            selfsync(sg)
            emit(sg, lambda E: E.scalar_tensor_tensor(
                vt[:, i:i + 1], sqt[:, i:i + 1], 1.0 / 1024.0, tmp1[:], ALU.mult, ALU.subtract),
                f"var{sfx}_{i}")
            wait("act", f"var{sfx}_{i}")
            emit("act", lambda E: E.activation(vt[:, i:i + 1], vt[:, i:i + 1], AF.Ln,
                                               bias=epscol[:]), None)
            selfsync("act")
            emit("act", lambda E: E.activation(rt[:, i:i + 1], vt[:, i:i + 1], AF.Exp,
                                               scale=-0.5), f"sd{sfx}_{i}")
            wait(sg, f"sd{sfx}_{i}")
            emit(sg, lambda E: E.scalar_tensor_tensor(
                nt[:, i:i + 1], mt[:, i:i + 1], -1.0, rt[:, i:i + 1], ALU.mult, ALU.mult),
                f"nmr{sfx}_{i}")

        def ln1(i):
            ln_stats("a", i, xbuf[i % 4], s1a, sqa, ma, va, ra, na, f"xld{i}")
            wait("pool", f"nmra_{i}", force=True)
            emit("pool", lambda E, i=i: E.tensor_scalar(
                V[:, 1024 * i:1024 * (i + 1)], xbuf[i % 4][:],
                ra[:, i:i + 1], na[:, i:i + 1], ALU.mult, ALU.add), f"ln1a_{i}")

        def ffn_tx(j):
            wait("pe", f"ln2a_{j}")
            if f"x1T_{j+1}" in ev:
                wait("pe", f"x1T_{j+1}")
            elif j >= 1:
                wait("pe", f"x2T_{j-1}")
            for c in range(8):
                emit("pe", lambda E, c=c: E.matmul(
                    TX_ps[:, 128 * c:128 * (c + 1)], xln2[:, 128 * c:128 * (c + 1)],
                    identbf[:], start=True, stop=True, is_transpose=True),
                    f"tx2_{j}" if c == 7 else None)
            wait("act", f"tx2_{j}")
            if f"projk_{j+1}" in ev:
                wait("act", f"projk_{j+1}")
            emit("act", lambda E: E.copy(xlnT[:], TX_ps[:]), f"x2T_{j}")

        def ffn_mm(j):
            wait("pe", f"x2T_{j}")
            wait("pe", "wf7")
            if j >= 1:
                wait("pe", f"t1_{j-1}")
            for c in range(8):
                for h in range(2):
                    emit("pe", lambda E, c=c, h=h: E.matmul(
                        Y_ps[:, 512 * h:512 * (h + 1)], xlnT[:, 128 * c:128 * (c + 1)],
                        Weff[:, 1024 * c + 512 * h:1024 * c + 512 * (h + 1)],
                        start=(c == 0), stop=False), None)
            wait("pe", "e_fb"); wait("pe", "c_onb")
            for h in range(2):
                emit("pe", lambda E, h=h: E.matmul(
                    Y_ps[:, 512 * h:512 * (h + 1)], onesbf[:],
                    biasrow[0:1, 512 * h:512 * (h + 1)], start=False, stop=True),
                    f"ffn_{j}" if h == 1 else None)

        def ffn_tail(j):
            wait("dve", f"ffn_{j}")
            wait("dve", f"x2_{j}")
            emit("dve", lambda E, j=j: E.tensor_tensor(
                xbuf[j % 4][:], Y_ps[:], x2buf[j % 2][:], ALU.add), f"t1_{j}")
            wait("pool", f"t1_{j}")
            dma("pool", lambda E, j=j: E.dma_start(
                out=out_d[128 * j:128 * (j + 1), :], in_=xbuf[j % 4][:]), "o", f"out_{j}")

        ln1(0)

        kser = 0
        ev_ept = [None]
        ev_avbuf = [None, None]
        def head_tx(j):
            wait("pe", f"ln1a_{j}")
            if j >= 2:
                wait("pe", f"x2T_{j-2}")
            for c in range(8):
                emit("pe", lambda E, j=j, c=c: E.matmul(
                    TX_ps[:, 128 * c:128 * (c + 1)],
                    V[:, 1024 * j + 128 * c:1024 * j + 128 * (c + 1)],
                    identbf[:], start=True, stop=True, is_transpose=True),
                    f"tx1_{j}" if c == 7 else None)
            wait("act", f"tx1_{j}")
            if j >= 2:
                wait("act", f"ffn_{j-2}")
            emit("act", lambda E: E.copy(xlnT[:], TX_ps[:]), f"x1T_{j}")

        def head_proj(j):
            wait("pe", f"x1T_{j}")
            if j >= 1:
                wait("pe", f"exp_{j-1}_{nch[j-1]-1}")
            for c in range(8):
                emit("pe", lambda E, c=c: E.matmul(
                    S_ps[0:16, 0:128], wqT[:, 16 * c:16 * (c + 1)],
                    xlnT[:, 128 * c:128 * (c + 1)], start=(c == 0), stop=(c == 7)),
                    f"projq_{j}" if c == 7 else None)
            for c in range(8):
                emit("pe", lambda E, c=c: E.matmul(
                    S_ps[0:16, 128:256], wkT[:, 16 * c:16 * (c + 1)],
                    xlnT[:, 128 * c:128 * (c + 1)], start=(c == 0), stop=(c == 7)),
                    f"projk_{j}" if c == 7 else None)
            wait("act", f"projq_{j}")
            wait("act", f"projk_{j}")
            if j >= 1:
                wait("act", f"S_{j-1}_{nch[j-1]-1}")
            emit("act", lambda E: E.activation(
                QTt[:], S_ps[0:16, 0:128], AF.Identity,
                bias=bqc[:], scale=0.25), f"eq_{j}")
            emit("act", lambda E, j=j: E.activation(
                KT[:, 128 * j:128 * (j + 1)], S_ps[0:16, 128:256], AF.Identity,
                bias=bkc[:]), f"ek_{j}")

        kb = [0]
        for i in range(T):
            kb.append(kb[-1] + nch[i])
        cpt = []
        for i in range(T):
            row = []
            for c in range(nch[i]):
                k0 = 1024 * c
                ncols = min(1024, 128 * (i + 1) - k0)
                row.append(dict(k0=k0, ncols=ncols, ktiles=ncols // 128,
                                km=(kb[i] + c) % 32, kmp=(kb[i] + c - 1) % 32,
                                diag=(c == nch[i] - 1)))
            cpt.append(row)

        def em_S(i, c):
            p = cpt[i][c]
            wait("pe", f"eq_{i}"); wait("pe", f"ek_{i}")
            if c >= 1:
                wait("pe", f"exp_{i}_{c-1}")
            nS = (p["ncols"] + 511) // 512
            for h in range(nS):
                w_ = min(512, p["ncols"] - 512 * h)
                emit("pe", lambda E, k0=p["k0"], h=h, w_=w_: E.matmul(
                    S_ps[:, 512 * h:512 * h + w_], QTt[:],
                    KT[:, k0 + 512 * h:k0 + 512 * h + w_], start=True, stop=True),
                    f"S_{i}_{c}" if h == nS - 1 else f"S_{i}_{c}_h0")

        def em_front(i, c):
            p = cpt[i][c]
            n = p["ncols"]
            km, kmp = p["km"], p["kmp"]
            pd = i % 8

            def red(lo, hi, ev_=None):
                emit("dve", lambda E, lo=lo, hi=hi: E.tensor_reduce(
                    mcol[:], S_ps[:, lo:hi], AX.X, ALU.max, negate=True), ev_)
                selfsync("dve")

            def fold(first, ev_=None):
                if first and c == 0:
                    emit("dve", lambda E, km=km: E.tensor_copy(nM[:, km:km + 1], mcol[:]), ev_)
                elif first:
                    emit("dve", lambda E, km=km, kmp=kmp: E.tensor_scalar(
                        nM[:, km:km + 1], mcol[:], nM[:, kmp:kmp + 1], None, ALU.min), ev_)
                else:
                    emit("dve", lambda E, km=km: E.tensor_scalar(
                        nM[:, km:km + 1], mcol[:], nM[:, km:km + 1], None, ALU.min), ev_)
                selfsync("dve")

            def msk():
                wait("dve", "c_msk")
                emit("dve", lambda E, pd=pd: E.tensor_tensor(
                    S_ps[:, 128 * pd:128 * (pd + 1)],
                    S_ps[:, 128 * pd:128 * (pd + 1)], maskneg[:], ALU.add),
                    f"msk_{i}_{c}")
                selfsync("dve")

            nme = f"nM_{i}_{c}"
            wait("dve", f"S_{i}_{c}")
            if p["diag"]:
                msk()
            red(0, n)
            fold(True, nme)
            if c >= 1:
                wait("act", f"nM_{i}_{c}")
                emit("act", lambda E, km=km, kmp=kmp: E.activation(
                    fc[:, km:km + 1], nM[:, kmp:kmp + 1], AF.Exp,
                    bias=nM[:, km:km + 1], scale=-1.0), f"fc_{i}_{c}")
            wait("act", f"nM_{i}_{c}")
            if c == 0 and i >= 1:
                wait("act", f"T_{i-1}_{nch[i-1]-1}")
            if c == 0 and i == 0:
                wait("act", "cas7")
            if c >= 1:
                wait("act", f"T_{i}_{c-1}")
            emit("act", lambda E, n=p["ncols"], km=p["km"]: E.activation(
                P[:, 0:n], S_ps[:, 0:n], AF.Exp,
                bias=nM[:, km:km + 1], accum_out=sc[:, km:km + 1]), f"exp_{i}_{c}")

        def em_T(i, k):
            wait("pe", f"exp_{i}_{k}")
            if ev_ept[0] is not None:
                wait("pe", ev_ept[0])
            for t in range(cpt[i][k]["ktiles"]):
                emit("pe", lambda E, t=t: E.matmul(
                    PT_ps[:, 128 * t:128 * (t + 1)], P[:, 128 * t:128 * (t + 1)],
                    identbf[:], start=True, stop=True, is_transpose=True),
                    f"T_{i}_{k}" if t == cpt[i][k]["ktiles"] - 1 else None)

        def em_ePT(i, k):
            wait("act", f"T_{i}_{k}")
            buf = k % 2
            n = cpt[i][k]["ncols"]
            if ev_avbuf[buf] is not None:
                wait("act", ev_avbuf[buf])
            emit("act", lambda E, n=n, buf=buf: E.copy(
                PTsb[buf][:, 0:n], PT_ps[:, 0:n]), f"ePT_{i}_{k}")
            ev_ept[0] = f"ePT_{i}_{k}"

        def em_rs(i, k):
            if k >= 1:
                wait("dve", f"fc_{i}_{k}")
                wait("dve", f"AV_{i}_{k-1}")
                emit("dve", lambda E, km=cpt[i][k]["km"]: E.tensor_scalar(
                    O_ps[:, 0:512], O_ps[:, 0:512], fc[:, km:km + 1], None, ALU.mult),
                    f"rsc0_{i}_{k}")
                emit("dve", lambda E, km=cpt[i][k]["km"]: E.tensor_scalar(
                    O_ps[:, 512:1024], O_ps[:, 512:1024], fc[:, km:km + 1], None, ALU.mult),
                    f"rsc_{i}_{k}")
            wait("dve", f"exp_{i}_{k}")
            if k >= 1:
                selfsync("dve")
            if k == 0:
                emit("dve", lambda E, i=i, km=cpt[i][k]["km"]: E.tensor_copy(
                    ssum[:, i:i + 1], sc[:, km:km + 1]), f"ssum_{i}_{k}")
            else:
                emit("dve", lambda E, i=i, km=cpt[i][k]["km"]: E.scalar_tensor_tensor(
                    ssum[:, i:i + 1], ssum[:, i:i + 1], fc[:, km:km + 1],
                    sc[:, km:km + 1], ALU.mult, ALU.add), f"ssum_{i}_{k}")

        def em_AV(i, k):
            p = cpt[i][k]
            if k == 0 and i == 0:
                wait("pe", "wf7")
            if k == 0 and i >= 1:
                wait("pe", f"tat_{i-1}")
            buf = k % 2
            for h in range(2):
                if k >= 1:
                    wait("pe", f"rsc0_{i}_{k}" if h == 0 else f"rsc_{i}_{k}")
                for t in range(p["ktiles"]):
                    wait("pe", f"ePT_{i}_{k}")
                    jj = 8 * k + t
                    last = (t == p["ktiles"] - 1 and h == 1)
                    emit("pe", lambda E, jj=jj, t=t, h=h, buf=buf,
                         st=(k == 0 and t == 0), sp=(p["diag"] and last): E.matmul(
                        O_ps[:, 512 * h:512 * (h + 1)], PTsb[buf][:, 128 * t:128 * (t + 1)],
                        V[:, 1024 * jj + 512 * h:1024 * jj + 512 * (h + 1)],
                        start=st, stop=sp, skip_group_check=True),
                        f"AV_{i}_{k}" if last else None)
            ev_avbuf[buf] = f"AV_{i}_{k}"

        head_tx(0)
        head_proj(0)
        em_S(0, 0)
        em_front(0, 0)
        for i in range(T):
            L = nch[i] - 1
            j = i + 2
            if 3 <= j < T:
                if j == 3:
                    wait("sync", "cas7")
                if f"out_{j-4}" in ev:
                    wait("sync", f"out_{j-4}")
                dma("sync", lambda E, j=j: E.dma_start(
                    out=xbuf[j % 4][:], in_=x_d[128 * j:128 * (j + 1), :]), "x", f"xld{j}")
            if i + 1 < T:
                ln1(i + 1)

            for c in range(1, nch[i]):
                em_S(i, c)
                if c == 1 and i >= 1:
                    ffn_tx(i - 1)
                em_T(i, c - 1)
                if c == 1 and i >= 1:
                    ffn_mm(i - 1)
                if c >= 2:
                    em_AV(i, c - 2)
                em_ePT(i, c - 1)
                em_front(i, c)
                em_rs(i, c - 1)
            em_T(i, L)
            if L >= 1:
                em_AV(i, L - 1)
            em_ePT(i, L)
            em_rs(i, L)
            selfsync("dve")
            emit("dve", lambda E, i=i: E.reciprocal(rec[:, i:i + 1], ssum[:, i:i + 1]), f"rec_{i}")
            if L == 0 and i >= 1:
                ffn_tx(i - 1)
                ffn_mm(i - 1)
            if i + 1 < T:
                head_tx(i + 1)
            em_AV(i, L)
            if i + 1 < T:
                head_proj(i + 1)
            ev[f"AV_{i}"] = ev[f"AV_{i}_{L}"]

            wait("act", f"rec_{i}")
            wait("act", f"AV_{i}")
            if i >= 1:
                wait("act", f"x2_{i-1}")
            emit("act", lambda E, i=i: E.activation(
                atn[:], O_ps[:], AF.Identity, scale=rec[:, i:i + 1],
                accum_out=satn[:]), f"tat_{i}")
            wait("pool", f"tat_{i}")
            if i >= 2:
                wait("pool", f"t1_{i-2}")
            emit("pool", lambda E, i=i: E.tensor_tensor(
                x2buf[i % 2][:], atn[:], xbuf[i % 4][:], ALU.add), f"x2_{i}")

            if i + 1 < T:
                em_S(i + 1, 0)
                em_front(i + 1, 0)
            if i >= 1:
                ffn_tail(i - 1)

            ln_stats("b", i, x2buf[i % 2], s1b, sqb, mb, vb, rb, nb, f"x2_{i}",
                     pre_sum=f"tat_{i}")

            selfsync("dve")
            if i >= 1:
                wait("dve", f"tx2_{i-1}")
            emit("dve", lambda E, i=i: E.tensor_scalar(
                xln2[:], x2buf[i % 2][:], rb[:, i:i + 1], nb[:, i:i + 1],
                ALU.mult, ALU.add), f"ln2a_{i}")


        ffn_tx(T - 1)
        ffn_mm(T - 1)
        ffn_tail(T - 1)
        c_o = dcnt["o"]
        sched.append(("pool", lambda E, c_o=c_o: E.wait_ge(dsem["o"], c_o)))

        def runner(engname):
            def body(E):
                for en, thunk in sched:
                    if en == engname:
                        thunk(E)
            return body

        block.sync(runner("sync"))
        block.tensor(runner("pe"))
        block.vector(runner("dve"))
        block.scalar(runner("act"))
        block.gpsimd(runner("pool"))

    return nc


def kernel(L0_attn_x, L0_attn_w_q, L0_attn_b_q, L0_attn_w_k, L0_attn_b_k,
           L0_ffn_kernel, L0_ffn_bias, L0_ffn_a, L0_ffn_b):
    x = np.ascontiguousarray(np.asarray(L0_attn_x, dtype=np.float32))
    wq = np.ascontiguousarray(np.asarray(L0_attn_w_q, dtype=np.float32))
    bq = np.ascontiguousarray(np.asarray(L0_attn_b_q, dtype=np.float32))
    wk = np.ascontiguousarray(np.asarray(L0_attn_w_k, dtype=np.float32))
    bk = np.ascontiguousarray(np.asarray(L0_attn_b_k, dtype=np.float32))
    kern = np.ascontiguousarray(np.asarray(L0_ffn_kernel, dtype=np.float32))
    fb = np.ascontiguousarray(np.asarray(L0_ffn_bias, dtype=np.float32))
    a = np.ascontiguousarray(np.asarray(L0_ffn_a, dtype=np.float32))
    b = np.ascontiguousarray(np.asarray(L0_ffn_b, dtype=np.float32))
    S = x.shape[1]
    nc = build_nc(S)
    in_maps = [{"x": np.ascontiguousarray(x[c]), "wq": wq, "bq": bq, "wk": wk, "bk": bk,
                "kern": kern, "fb": fb, "a": a, "b": b} for c in range(x.shape[0])]
    res = run_bass_kernel_spmd(nc, in_maps, core_ids=list(range(x.shape[0])))
    return np.stack([r["out"] for r in res.results], axis=0)


# version 60
# speedup vs baseline: 1.0379x; 1.0137x over previous
import numpy as np
import ml_dtypes
from contextlib import ExitStack

import concourse.bass as bass
import concourse.mybir as mybir
from concourse.bass_utils import run_bass_kernel_spmd

F32 = mybir.dt.float32
BF16 = mybir.dt.bfloat16
AF = mybir.ActivationFunctionType
ALU = mybir.AluOpType
AX = mybir.AxisListType

D = 1024


def build_nc(S):
    T = S // 128
    nch = [(i + 8) // 8 for i in range(T)]
    nc = bass.Bass("TRN2", target_bir_lowering=False, debug=False)

    x_d = nc.dram_tensor("x", [S, D], F32, kind="ExternalInput")
    wq_d = nc.dram_tensor("wq", [16, D], F32, kind="ExternalInput")
    bq_d = nc.dram_tensor("bq", [16], F32, kind="ExternalInput")
    wk_d = nc.dram_tensor("wk", [16, D], F32, kind="ExternalInput")
    bk_d = nc.dram_tensor("bk", [16], F32, kind="ExternalInput")
    kern_d = nc.dram_tensor("kern", [D, D], F32, kind="ExternalInput")
    fb_d = nc.dram_tensor("fb", [D], F32, kind="ExternalInput")
    a_d = nc.dram_tensor("a", [D, 16], F32, kind="ExternalInput")
    b_d = nc.dram_tensor("b", [16, D], F32, kind="ExternalInput")
    out_d = nc.dram_tensor("out", [S, D], F32, kind="ExternalOutput")

    identbf_d = nc.inline_tensor(np.eye(128, dtype=ml_dtypes.bfloat16), name="identbf_c")
    identf_d = nc.inline_tensor(np.eye(128, dtype=np.float32), name="identf_c")
    maskneg_d = nc.inline_tensor(
        np.triu(np.full((128, 128), -1.0e6, np.float32), 1), name="maskneg_c")
    onesf_d = nc.inline_tensor(np.ones((1, 128), np.float32), name="onesf_c")
    onesbf_d = nc.inline_tensor(np.ones((1, 128), ml_dtypes.bfloat16), name="onesbf_c")

    with ExitStack() as st:
        e = st.enter_context
        V = e(nc.sbuf_tensor("V", [128, T * 1024], BF16))
        QTt = e(nc.sbuf_tensor("QTt", [16, 128], BF16))
        KT = e(nc.sbuf_tensor("KT", [16, S], BF16))
        Weff = e(nc.sbuf_tensor("Weff", [128, 8 * 1024], BF16))
        xbuf = [e(nc.sbuf_tensor(f"xbuf{j}", [128, 1024], F32)) for j in range(3)]
        x2buf = [e(nc.sbuf_tensor(f"x2buf{j}", [128, 1024], BF16)) for j in range(2)]
        P = e(nc.sbuf_tensor("P", [128, 1024], BF16))
        atn = e(nc.sbuf_tensor("atn", [128, 1024], BF16))
        PTsb = [e(nc.sbuf_tensor(f"PTsb{j}", [128, 1024], BF16)) for j in range(2)]
        xlnT = e(nc.sbuf_tensor("xlnT", [128, 1024], BF16))
        xln2 = e(nc.sbuf_tensor("xln2", [128, 1024], BF16))
        kst = [e(nc.sbuf_tensor(f"kst{j}", [128, 1024], F32)) for j in range(2)]
        xbuf.append(kst[1])
        aT = e(nc.sbuf_tensor("aT", [16, 1024], BF16))
        bsb = e(nc.sbuf_tensor("bsb", [16, 1024], BF16))
        biasrow = aT
        identbf = e(nc.sbuf_tensor("identbf", [128, 128], BF16))
        identf = e(nc.sbuf_tensor("identf", [128, 128], F32))
        maskneg = e(nc.sbuf_tensor("maskneg", [128, 128], F32))
        onesf = e(nc.sbuf_tensor("onesf", [1, 128], F32))
        onesbf = e(nc.sbuf_tensor("onesbf", [1, 128], BF16))
        wqT = e(nc.sbuf_tensor("wqT", [128, 128], BF16))
        wkT = e(nc.sbuf_tensor("wkT", [128, 128], BF16))
        s1a = e(nc.sbuf_tensor("s1a", [128, T], F32))
        sqa = e(nc.sbuf_tensor("sqa", [128, T], F32))
        ma = e(nc.sbuf_tensor("ma", [128, T], F32))
        va = e(nc.sbuf_tensor("va", [128, T], F32))
        ra = e(nc.sbuf_tensor("ra", [128, T], F32))
        na = e(nc.sbuf_tensor("na", [128, T], F32))
        s1b = e(nc.sbuf_tensor("s1b", [128, T], F32))
        sqb = e(nc.sbuf_tensor("sqb", [128, T], F32))
        mb = e(nc.sbuf_tensor("mb", [128, T], F32))
        vb = e(nc.sbuf_tensor("vb", [128, T], F32))
        rb = e(nc.sbuf_tensor("rb", [128, T], F32))
        nb = e(nc.sbuf_tensor("nb", [128, T], F32))
        ssum = e(nc.sbuf_tensor("ssum", [128, T], F32))
        rec = e(nc.sbuf_tensor("rec", [128, T], F32))
        satn = e(nc.sbuf_tensor("satn", [128, 1], F32))
        nM = e(nc.sbuf_tensor("nM", [128, 32], F32))
        fc = e(nc.sbuf_tensor("fc", [128, 32], F32))
        sc = e(nc.sbuf_tensor("sc", [128, 32], F32))
        mcol = e(nc.sbuf_tensor("mcol", [128, 1], F32))
        tmp1 = e(nc.sbuf_tensor("tmp1", [128, 1], F32))
        epscol = e(nc.sbuf_tensor("epscol", [128, 1], F32))
        amax8 = e(nc.sbuf_tensor("amax8", [128, 8], F32))
        sp = e(nc.sbuf_tensor("sp", [1, 8], F32))
        s4 = e(nc.sbuf_tensor("s4", [128, 2], F32))
        scl = e(nc.sbuf_tensor("scl", [128, 2], F32))
        th = e(nc.sbuf_tensor("th", [128, 16], F32))
        bqc = e(nc.sbuf_tensor("bqc", [16, 1], F32))
        bkc = e(nc.sbuf_tensor("bkc", [16, 1], F32))

        S_ps = e(nc.psum_tensor("S_ps", [128, 1024], F32))
        O_ps = e(nc.psum_tensor("O_ps", [128, 1024], F32))
        Y_ps = e(nc.psum_tensor("Y_ps", [128, 1024], F32))
        PT_ps = e(nc.psum_tensor("PT_ps", [128, 1024], BF16))
        TX_ps = e(nc.psum_tensor("TX_ps", [128, 1024], BF16))

        sem = {k: e(nc.semaphore(f"s_{k}")) for k in ["pe", "act", "dve", "pool", "sync"]}
        dsem = {k: e(nc.semaphore(f"d_{k}")) for k in ["x", "w", "o", "w2", "w4", "x2", "w3"]}
        block = e(nc.Block())

        sched = []
        cnt = {k: 0 for k in sem}
        dcnt = {k: 0 for k in dsem}
        ev = {}
        lastw = {k: {} for k in sem}

        def emit(eng, fn, name=None):
            semh = sem[eng]
            def thunk(E, fn=fn, semh=semh):
                fn(E).then_inc(semh, 1)
            sched.append((eng, thunk))
            cnt[eng] += 1
            if name is not None:
                ev[name] = (eng, semh, cnt[eng])

        def dma(eng, fn, dname, name=None):
            semh = dsem[dname]
            prev = dcnt[dname]
            if prev > 0:
                lw = lastw[eng]
                if lw.get(id(semh), -1) < prev:
                    lw[id(semh)] = prev
                    sched.append((eng, lambda E, semh=semh, prev=prev: E.wait_ge(semh, prev)))
            def thunk(E, fn=fn, semh=semh):
                fn(E).then_inc(semh, 16)
            sched.append((eng, thunk))
            dcnt[dname] += 16
            if name is not None:
                ev[name] = ("dma", semh, dcnt[dname])

        def wait(eng, name, force=False):
            src, semh, c = ev[name]
            if src == eng and not force:
                return
            lw = lastw[eng]
            if lw.get(id(semh), -1) >= c:
                return
            lw[id(semh)] = c
            sched.append((eng, lambda E, semh=semh, c=c: E.wait_ge(semh, c)))

        def selfsync(eng):
            semh, c = sem[eng], cnt[eng]
            lw = lastw[eng]
            if lw.get(id(semh), -1) >= c:
                return
            lw[id(semh)] = c
            sched.append((eng, lambda E, semh=semh, c=c: E.wait_ge(semh, c)))

        dma("sync", lambda E: E.dma_start(out=identf[:], in_=identf_d[:]), "w", "c_idf")
        dma("sync", lambda E: E.dma_start(out=identbf[:], in_=identbf_d[:]), "w2", "c_idb")
        dma("sync", lambda E: E.dma_start(out=maskneg[:], in_=maskneg_d[:]), "w2", "c_msk")
        dma("sync", lambda E: E.dma_start(out=onesf[:], in_=onesf_d[:]), "w3", "c_onf")
        dma("sync", lambda E: E.dma_start(out=onesbf[:], in_=onesbf_d[:]), "w3", "c_onb")
        dma("sync", lambda E: E.dma_start(out=bqc[:], in_=bq_d[:].rearrange("(e o) -> e o", o=1)), "x", "c_bq")
        dma("sync", lambda E: E.dma_start(out=bkc[:], in_=bk_d[:].rearrange("(e o) -> e o", o=1)), "x", "c_bk")

        emit("dve", lambda E: E.memset(epscol[:], 1e-5), "eps")
        wait("dve", "c_bq")
        emit("dve", lambda E: E.tensor_scalar(bqc[:], bqc[:], 0.25, None, ALU.mult), "bq4")

        dma("sync", lambda E: E.dma_start(out=xbuf[1][0:16, :], in_=wq_d[:]), "w", "ld_wq")
        dma("sync", lambda E: E.dma_start(out=xbuf[2][0:16, :], in_=wk_d[:]), "w", "ld_wk")
        wait("pe", "ld_wq"); wait("pe", "c_idf")
        for c in range(8):
            emit("pe", lambda E, c=c: E.matmul(
                S_ps[:, 16 * c:16 * (c + 1)], xbuf[1][0:16, 128 * c:128 * (c + 1)],
                identf[0:16, 0:16], start=True, stop=True, is_transpose=True),
                "t_wq" if c == 7 else None)
        wait("act", "t_wq")
        emit("act", lambda E: E.copy(wqT[:], S_ps[:, 0:128]), "e_wq")
        dma("act", lambda E: E.dma_start(
            out=xbuf[1][:, 0:128].rearrange("p (c r) -> p c r", c=8),
            in_=a_d[:].rearrange("(c p) r -> p c r", p=128)), "w2", "ld_a")
        wait("pe", "ld_wk")
        for c in range(8):
            emit("pe", lambda E, c=c: E.matmul(
                S_ps[:, 128 + 16 * c:128 + 16 * (c + 1)], xbuf[2][0:16, 128 * c:128 * (c + 1)],
                identf[0:16, 0:16], start=True, stop=True, is_transpose=True),
                "t_wk" if c == 7 else None)
        wait("act", "t_wk")
        emit("act", lambda E: E.copy(wkT[:], S_ps[:, 128:256]), "e_wk")
        dma("act", lambda E: E.dma_start(out=xbuf[2][0:16, :], in_=b_d[:]), "w2", "ld_b")
        wait("pe", "ld_a"); wait("pe", "e_wk")
        for c in range(8):
            emit("pe", lambda E, c=c: E.matmul(
                S_ps[0:16, 128 * c:128 * (c + 1)], xbuf[1][:, 16 * c:16 * (c + 1)],
                identf[:], start=True, stop=True, is_transpose=True),
                "t_a" if c == 7 else None)
        wait("act", "t_a")
        emit("act", lambda E: E.copy(aT[:], S_ps[0:16, 0:1024]), "e_aT")
        def kload(c, tag, waitev):
            buf = kst[c % 2]
            qeng, qsem = [("sync", "w"), ("act", "w2")][c % 2]
            if waitev is not None:
                wait(qeng, waitev)
            dma(qeng, lambda E, c=c, buf=buf: E.dma_start(
                out=buf[:], in_=kern_d[128 * c:128 * (c + 1), :]),
                qsem, f"{tag}{c}_0")
            ev[f"{tag}{c}_1"] = ev[f"{tag}{c}_0"]

        for c in range(8):
            kload(c, "kld", f"am{c - 2}" if c >= 2 else None)
            wait("dve", f"kld{c}_0"); wait("dve", f"kld{c}_1")
            emit("dve", lambda E, c=c: E.tensor_reduce(
                amax8[:, c:c + 1], kst[c % 2][:], AX.X, ALU.max, apply_absolute_value=True), f"am{c}")
        dma("sync", lambda E: E.dma_start(out=xbuf[0][:], in_=x_d[0:128, :]), "x", "xld0")
        wait("sync", "t_a"); wait("sync", "am7")
        dma("sync", lambda E: E.dma_start(out=xbuf[1][:], in_=x_d[128:256, :]), "w3", "xld1")
        selfsync("dve")
        emit("dve", lambda E: E.tensor_reduce(mcol[:], amax8[:], AX.X, ALU.max), "am_f")
        wait("pe", "am_f"); wait("pe", "e_aT")
        emit("pe", lambda E: E.matmul(S_ps[0:1, 0:128], mcol[:], identf[:],
                                      start=True, stop=True, is_transpose=True), "am_t")
        wait("dve", "am_t")
        emit("dve", lambda E: E.tensor_reduce(sp[0:1, 0:1], S_ps[0:1, 0:128], AX.X, ALU.max), "am_s")
        selfsync("dve")
        emit("dve", lambda E: E.tensor_scalar(sp[0:1, 1:2], sp[0:1, 0:1], 1.0 / 7.0, None, ALU.mult))
        emit("dve", lambda E: E.reciprocal(sp[0:1, 3:4], sp[0:1, 0:1]))
        selfsync("dve")
        emit("dve", lambda E: E.tensor_scalar(sp[0:1, 2:3], sp[0:1, 3:4], 7.0, None, ALU.mult), "am_rs")
        wait("pe", "am_rs"); wait("pe", "c_onf")
        emit("pe", lambda E: E.matmul(S_ps[:, 0:2], onesf[:], sp[0:1, 1:3],
                                      start=True, stop=True), "bc_mm")
        wait("dve", "bc_mm")
        emit("dve", lambda E: E.tensor_copy(scl[:], S_ps[:, 0:2]), "e_scl")
        selfsync("dve")
        for j in range(14):
            emit("dve", lambda E, j=j: E.tensor_scalar(
                th[:, j:j + 1], scl[:, 0:1], float(j) - 6.5, None, ALU.mult))
        emit("dve", lambda E: E.tensor_scalar(th[:, 14:15], scl[:, 0:1], -7.0, None, ALU.mult), "e_th")
        emit("dve", lambda E: E.tensor_scalar(s4[:, 0:1], scl[:, 0:1], -4.0, None, ALU.mult), None)
        emit("dve", lambda E: E.memset(s4[:, 1:2], 4.0), "e_s4")
        wait("dve", "ld_b")
        emit("dve", lambda E: E.tensor_scalar(bsb[:], xbuf[2][0:16, :], scl[0:16, 1:2], None, ALU.mult),
             "e_bsb")
        selfsync("dve")

        for c in range(8):
            kload(c, "k2ld", "am7" if c < 2 else f"cas{c - 2}")
            if c == 1:
                wait("sync", "e_bsb")
                dma("sync", lambda E: E.dma_start(out=xbuf[2][:], in_=x_d[256:384, :]), "x", "xld2")
            ps = [Y_ps, O_ps][c % 2]
            wait("pe", "e_bsb"); wait("pe", "e_aT")
            if c >= 2:
                wait("pe", f"wf{c - 2}")
            for h in range(2):
                emit("pe", lambda E, c=c, h=h, ps=ps: E.matmul(
                    ps[:, 512 * h:512 * (h + 1)], aT[:, 128 * c:128 * (c + 1)],
                    bsb[0:16, 512 * h:512 * (h + 1)], start=True, stop=True),
                    f"lora{c}" if h == 1 else None)
            buf = kst[c % 2]
            wait("dve", f"k2ld{c}_0"); wait("dve", f"k2ld{c}_1")
            wait("dve", f"lora{c}")
            emit("dve", lambda E, buf=buf: E.tensor_scalar(
                P[:], buf[:], th[:, 3:4], None, ALU.is_ge), None)
            selfsync("dve")
            for jj in (7, 11):
                emit("dve", lambda E, buf=buf, jj=jj: E.scalar_tensor_tensor(
                    P[:], buf[:], th[:, jj:jj + 1], P[:], ALU.is_ge, ALU.add), None)
                selfsync("dve")
            emit("dve", lambda E, buf=buf: E.scalar_tensor_tensor(
                buf[:], P[:], s4[:, 0:1], buf[:], ALU.mult, ALU.add), None)
            selfsync("dve")
            emit("dve", lambda E, ps=ps: E.scalar_tensor_tensor(
                ps[:], P[:], s4[:, 1:2], ps[:], ALU.mult, ALU.add), None)
            selfsync("dve")
            for jj in (0, 1):
                emit("dve", lambda E, buf=buf, jj=jj, ps=ps: E.scalar_tensor_tensor(
                    ps[:], buf[:], th[:, jj:jj + 1], ps[:], ALU.is_ge, ALU.add), None)
                selfsync("dve")
            emit("dve", lambda E, buf=buf, ps=ps: E.scalar_tensor_tensor(
                ps[:], buf[:], th[:, 2:3], ps[:], ALU.is_ge, ALU.add), f"cas{c}")
            selfsync("dve")
            emit("dve", lambda E, c=c, ps=ps: E.tensor_scalar(
                Weff[:, 1024 * c:1024 * (c + 1)], ps[:], scl[:, 0:1], th[:, 14:15],
                ALU.mult, ALU.add), f"wf{c}")

        wait("sync", "cas6")
        dma("sync", lambda E: E.dma_start(out=kst[0][0:1, :], in_=fb_d[:].rearrange("(o f) -> o f", o=1)),
            "w", "ld_fb")
        wait("dve", "ld_fb")
        wait("dve", "lora7")
        emit("dve", lambda E: E.tensor_copy(aT[0:1, :], kst[0][0:1, :]), "e_fb")

        def ln_stats(sfx, i, src, s1t, sqt, mt, vt, rt, nt, ready_ev, pre_sum=None):
            sg = "dve"
            if pre_sum is not None:
                wait(sg, pre_sum)
                emit(sg, lambda E, i=i: E.tensor_tensor(
                    s1t[:, i:i + 1], s1a[:, i:i + 1], satn[:], ALU.add), f"rs{sfx}_{i}")
                selfsync(sg)
                wait(sg, ready_ev, force=True)
                wait(sg, "e_fb")
            else:
                wait(sg, ready_ev, force=True)
                wait(sg, "e_fb")
                emit(sg, lambda E: E.reduce_sum(s1t[:, i:i + 1], src[:], AX.X), f"rs{sfx}_{i}")
                selfsync(sg)
            emit(sg, lambda E: E.scalar_tensor_tensor(
                kst[0][:], src[:], 0.0, src[:], ALU.bypass, ALU.mult,
                accum_out=sqt[:, i:i + 1]), f"sq{sfx}_{i}")
            selfsync(sg)
            emit(sg, lambda E: E.tensor_scalar(mt[:, i:i + 1], s1t[:, i:i + 1],
                                               1.0 / 1024.0, None, ALU.mult))
            selfsync(sg)
            emit(sg, lambda E: E.tensor_tensor(tmp1[:], mt[:, i:i + 1], mt[:, i:i + 1], ALU.mult))
            selfsync(sg)
            emit(sg, lambda E: E.scalar_tensor_tensor(
                vt[:, i:i + 1], sqt[:, i:i + 1], 1.0 / 1024.0, tmp1[:], ALU.mult, ALU.subtract),
                f"var{sfx}_{i}")
            wait("act", f"var{sfx}_{i}")
            emit("act", lambda E: E.activation(vt[:, i:i + 1], vt[:, i:i + 1], AF.Ln,
                                               bias=epscol[:]), None)
            selfsync("act")
            emit("act", lambda E: E.activation(rt[:, i:i + 1], vt[:, i:i + 1], AF.Exp,
                                               scale=-0.5), f"sd{sfx}_{i}")
            wait(sg, f"sd{sfx}_{i}")
            emit(sg, lambda E: E.scalar_tensor_tensor(
                nt[:, i:i + 1], mt[:, i:i + 1], -1.0, rt[:, i:i + 1], ALU.mult, ALU.mult),
                f"nmr{sfx}_{i}")

        def ln1(i):
            ln_stats("a", i, xbuf[i % 4], s1a, sqa, ma, va, ra, na, f"xld{i}")
            wait("pool", f"nmra_{i}", force=True)
            emit("pool", lambda E, i=i: E.tensor_scalar(
                V[:, 1024 * i:1024 * (i + 1)], xbuf[i % 4][:],
                ra[:, i:i + 1], na[:, i:i + 1], ALU.mult, ALU.add), f"ln1a_{i}")

        def ffn_tx(j):
            wait("pe", f"ln2a_{j}")
            if f"x1T_{j+1}" in ev:
                wait("pe", f"x1T_{j+1}")
            elif j >= 1:
                wait("pe", f"x2T_{j-1}")
            for c in range(8):
                emit("pe", lambda E, c=c: E.matmul(
                    TX_ps[:, 128 * c:128 * (c + 1)], xln2[:, 128 * c:128 * (c + 1)],
                    identbf[:], start=True, stop=True, is_transpose=True),
                    f"tx2_{j}" if c == 7 else None)
            wait("act", f"tx2_{j}")
            if f"projk_{j+1}" in ev:
                wait("act", f"projk_{j+1}")
            emit("act", lambda E: E.copy(xlnT[:], TX_ps[:]), f"x2T_{j}")

        def ffn_mm(j):
            wait("pe", f"x2T_{j}")
            wait("pe", "wf7")
            if j >= 1:
                wait("pe", f"t1_{j-1}")
            for c in range(8):
                for h in range(2):
                    emit("pe", lambda E, c=c, h=h: E.matmul(
                        Y_ps[:, 512 * h:512 * (h + 1)], xlnT[:, 128 * c:128 * (c + 1)],
                        Weff[:, 1024 * c + 512 * h:1024 * c + 512 * (h + 1)],
                        start=(c == 0), stop=False), None)
            wait("pe", "e_fb"); wait("pe", "c_onb")
            for h in range(2):
                emit("pe", lambda E, h=h: E.matmul(
                    Y_ps[:, 512 * h:512 * (h + 1)], onesbf[:],
                    biasrow[0:1, 512 * h:512 * (h + 1)], start=False, stop=True),
                    f"ffn_{j}" if h == 1 else None)

        def ffn_tail(j):
            wait("dve", f"ffn_{j}")
            wait("dve", f"x2_{j}")
            emit("dve", lambda E, j=j: E.tensor_tensor(
                xbuf[j % 4][:], Y_ps[:], x2buf[j % 2][:], ALU.add), f"t1_{j}")
            wait("pool", f"t1_{j}")
            dma("pool", lambda E, j=j: E.dma_start(
                out=out_d[128 * j:128 * (j + 1), :], in_=xbuf[j % 4][:]), "o", f"out_{j}")

        ln1(0)

        kser = 0
        ev_ept = [None]
        ev_avbuf = [None, None]
        def head_tx(j):
            wait("pe", f"ln1a_{j}")
            if j >= 2:
                wait("pe", f"x2T_{j-2}")
            for c in range(8):
                emit("pe", lambda E, j=j, c=c: E.matmul(
                    TX_ps[:, 128 * c:128 * (c + 1)],
                    V[:, 1024 * j + 128 * c:1024 * j + 128 * (c + 1)],
                    identbf[:], start=True, stop=True, is_transpose=True),
                    f"tx1_{j}" if c == 7 else None)
            wait("act", f"tx1_{j}")
            if j >= 2:
                wait("act", f"ffn_{j-2}")
            emit("act", lambda E: E.copy(xlnT[:], TX_ps[:]), f"x1T_{j}")

        def head_proj(j):
            wait("pe", f"x1T_{j}")
            if j >= 1:
                wait("pe", f"exp_{j-1}_{nch[j-1]-1}")
            for c in range(8):
                emit("pe", lambda E, c=c: E.matmul(
                    S_ps[0:16, 0:128], wqT[:, 16 * c:16 * (c + 1)],
                    xlnT[:, 128 * c:128 * (c + 1)], start=(c == 0), stop=(c == 7)),
                    f"projq_{j}" if c == 7 else None)
            for c in range(8):
                emit("pe", lambda E, c=c: E.matmul(
                    S_ps[0:16, 128:256], wkT[:, 16 * c:16 * (c + 1)],
                    xlnT[:, 128 * c:128 * (c + 1)], start=(c == 0), stop=(c == 7)),
                    f"projk_{j}" if c == 7 else None)
            wait("act", f"projq_{j}")
            wait("act", f"projk_{j}")
            if j >= 1:
                wait("act", f"S_{j-1}_{nch[j-1]-1}")
            emit("act", lambda E: E.activation(
                QTt[:], S_ps[0:16, 0:128], AF.Identity,
                bias=bqc[:], scale=0.25), f"eq_{j}")
            emit("act", lambda E, j=j: E.activation(
                KT[:, 128 * j:128 * (j + 1)], S_ps[0:16, 128:256], AF.Identity,
                bias=bkc[:]), f"ek_{j}")

        kb = [0]
        for i in range(T):
            kb.append(kb[-1] + nch[i])
        cpt = []
        for i in range(T):
            row = []
            for c in range(nch[i]):
                k0 = 1024 * c
                ncols = min(1024, 128 * (i + 1) - k0)
                row.append(dict(k0=k0, ncols=ncols, ktiles=ncols // 128,
                                km=(kb[i] + c) % 32, kmp=(kb[i] + c - 1) % 32,
                                diag=(c == nch[i] - 1)))
            cpt.append(row)

        def em_S(i, c):
            p = cpt[i][c]
            wait("pe", f"eq_{i}"); wait("pe", f"ek_{i}")
            if c >= 1:
                wait("pe", f"exp_{i}_{c-1}")
            nS = (p["ncols"] + 511) // 512
            for h in range(nS):
                w_ = min(512, p["ncols"] - 512 * h)
                emit("pe", lambda E, k0=p["k0"], h=h, w_=w_: E.matmul(
                    S_ps[:, 512 * h:512 * h + w_], QTt[:],
                    KT[:, k0 + 512 * h:k0 + 512 * h + w_], start=True, stop=True),
                    f"S_{i}_{c}" if h == nS - 1 else f"S_{i}_{c}_h0")

        def em_front(i, c):
            p = cpt[i][c]
            n = p["ncols"]
            km, kmp = p["km"], p["kmp"]
            pd = i % 8

            def red(lo, hi, ev_=None):
                emit("dve", lambda E, lo=lo, hi=hi: E.tensor_reduce(
                    mcol[:], S_ps[:, lo:hi], AX.X, ALU.max, negate=True), ev_)
                selfsync("dve")

            def fold(first, ev_=None):
                if first and c == 0:
                    emit("dve", lambda E, km=km: E.tensor_copy(nM[:, km:km + 1], mcol[:]), ev_)
                elif first:
                    emit("dve", lambda E, km=km, kmp=kmp: E.tensor_scalar(
                        nM[:, km:km + 1], mcol[:], nM[:, kmp:kmp + 1], None, ALU.min), ev_)
                else:
                    emit("dve", lambda E, km=km: E.tensor_scalar(
                        nM[:, km:km + 1], mcol[:], nM[:, km:km + 1], None, ALU.min), ev_)
                selfsync("dve")

            def msk():
                wait("dve", "c_msk")
                emit("dve", lambda E, pd=pd: E.tensor_tensor(
                    S_ps[:, 128 * pd:128 * (pd + 1)],
                    S_ps[:, 128 * pd:128 * (pd + 1)], maskneg[:], ALU.add),
                    f"msk_{i}_{c}")
                selfsync("dve")

            nme = f"nM_{i}_{c}"
            wait("dve", f"S_{i}_{c}")
            if p["diag"]:
                msk()
            red(0, n)
            fold(True, nme)
            if c >= 1:
                wait("act", f"nM_{i}_{c}")
                emit("act", lambda E, km=km, kmp=kmp: E.activation(
                    fc[:, km:km + 1], nM[:, kmp:kmp + 1], AF.Exp,
                    bias=nM[:, km:km + 1], scale=-1.0), f"fc_{i}_{c}")
            wait("act", f"nM_{i}_{c}")
            if c == 0 and i >= 1:
                wait("act", f"T_{i-1}_{nch[i-1]-1}")
            if c == 0 and i == 0:
                wait("act", "cas7")
            if c >= 1:
                wait("act", f"T_{i}_{c-1}")
            emit("act", lambda E, n=p["ncols"], km=p["km"]: E.activation(
                P[:, 0:n], S_ps[:, 0:n], AF.Exp,
                bias=nM[:, km:km + 1], accum_out=sc[:, km:km + 1]), f"exp_{i}_{c}")

        def em_T(i, k):
            wait("pe", f"exp_{i}_{k}")
            if ev_ept[0] is not None:
                wait("pe", ev_ept[0])
            for t in range(cpt[i][k]["ktiles"]):
                emit("pe", lambda E, t=t: E.matmul(
                    PT_ps[:, 128 * t:128 * (t + 1)], P[:, 128 * t:128 * (t + 1)],
                    identbf[:], start=True, stop=True, is_transpose=True),
                    f"T_{i}_{k}" if t == cpt[i][k]["ktiles"] - 1 else None)

        def em_ePT(i, k):
            wait("act", f"T_{i}_{k}")
            buf = k % 2
            n = cpt[i][k]["ncols"]
            if ev_avbuf[buf] is not None:
                wait("act", ev_avbuf[buf])
            emit("act", lambda E, n=n, buf=buf: E.copy(
                PTsb[buf][:, 0:n], PT_ps[:, 0:n]), f"ePT_{i}_{k}")
            ev_ept[0] = f"ePT_{i}_{k}"

        def em_rs(i, k):
            if k >= 1:
                wait("dve", f"fc_{i}_{k}")
                wait("dve", f"AV_{i}_{k-1}")
                emit("dve", lambda E, km=cpt[i][k]["km"]: E.tensor_scalar(
                    O_ps[:, 0:512], O_ps[:, 0:512], fc[:, km:km + 1], None, ALU.mult),
                    f"rsc0_{i}_{k}")
                emit("dve", lambda E, km=cpt[i][k]["km"]: E.tensor_scalar(
                    O_ps[:, 512:1024], O_ps[:, 512:1024], fc[:, km:km + 1], None, ALU.mult),
                    f"rsc_{i}_{k}")
            wait("dve", f"exp_{i}_{k}")
            if k >= 1:
                selfsync("dve")
            if k == 0:
                emit("dve", lambda E, i=i, km=cpt[i][k]["km"]: E.tensor_copy(
                    ssum[:, i:i + 1], sc[:, km:km + 1]), f"ssum_{i}_{k}")
            else:
                emit("dve", lambda E, i=i, km=cpt[i][k]["km"]: E.scalar_tensor_tensor(
                    ssum[:, i:i + 1], ssum[:, i:i + 1], fc[:, km:km + 1],
                    sc[:, km:km + 1], ALU.mult, ALU.add), f"ssum_{i}_{k}")

        def em_AV(i, k):
            p = cpt[i][k]
            if k == 0 and i == 0:
                wait("pe", "wf7")
            if k == 0 and i >= 1:
                wait("pe", f"tat_{i-1}")
            buf = k % 2
            for h in range(2):
                if k >= 1:
                    wait("pe", f"rsc0_{i}_{k}" if h == 0 else f"rsc_{i}_{k}")
                for t in range(p["ktiles"]):
                    wait("pe", f"ePT_{i}_{k}")
                    jj = 8 * k + t
                    last = (t == p["ktiles"] - 1 and h == 1)
                    emit("pe", lambda E, jj=jj, t=t, h=h, buf=buf,
                         st=(k == 0 and t == 0), sp=(p["diag"] and last): E.matmul(
                        O_ps[:, 512 * h:512 * (h + 1)], PTsb[buf][:, 128 * t:128 * (t + 1)],
                        V[:, 1024 * jj + 512 * h:1024 * jj + 512 * (h + 1)],
                        start=st, stop=sp, skip_group_check=True),
                        f"AV_{i}_{k}" if last else None)
            ev_avbuf[buf] = f"AV_{i}_{k}"

        head_tx(0)
        head_proj(0)
        em_S(0, 0)
        em_front(0, 0)
        for i in range(T):
            L = nch[i] - 1
            j = i + 2
            if 3 <= j < T:
                if j == 3:
                    wait("sync", "cas7")
                if f"out_{j-4}" in ev:
                    wait("sync", f"out_{j-4}")
                dma("sync", lambda E, j=j: E.dma_start(
                    out=xbuf[j % 4][:], in_=x_d[128 * j:128 * (j + 1), :]), "x", f"xld{j}")
            if i + 1 < T:
                ln1(i + 1)

            for c in range(1, nch[i]):
                em_S(i, c)
                if c == 1 and i >= 1:
                    ffn_tx(i - 1)
                em_T(i, c - 1)
                if c == 1 and i >= 1:
                    ffn_mm(i - 1)
                if c >= 2:
                    em_AV(i, c - 2)
                em_ePT(i, c - 1)
                em_front(i, c)
                em_rs(i, c - 1)
            em_T(i, L)
            if L >= 1:
                em_AV(i, L - 1)
            em_ePT(i, L)
            em_rs(i, L)
            selfsync("dve")
            emit("dve", lambda E, i=i: E.reciprocal(rec[:, i:i + 1], ssum[:, i:i + 1]), f"rec_{i}")
            if L == 0 and i >= 1:
                ffn_tx(i - 1)
                ffn_mm(i - 1)
            if i + 1 < T:
                head_tx(i + 1)
            em_AV(i, L)
            if i + 1 < T:
                head_proj(i + 1)
            ev[f"AV_{i}"] = ev[f"AV_{i}_{L}"]

            wait("act", f"rec_{i}")
            wait("act", f"AV_{i}")
            if i >= 1:
                wait("act", f"x2_{i-1}")
            emit("act", lambda E, i=i: E.activation(
                atn[:], O_ps[:], AF.Identity, scale=rec[:, i:i + 1],
                accum_out=satn[:]), f"tat_{i}")
            wait("pool", f"tat_{i}")
            if i >= 2:
                wait("pool", f"t1_{i-2}")
            emit("pool", lambda E, i=i: E.tensor_tensor(
                x2buf[i % 2][:], atn[:], xbuf[i % 4][:], ALU.add), f"x2_{i}")

            if i + 1 < T:
                em_S(i + 1, 0)
                em_front(i + 1, 0)
            if i >= 1:
                ffn_tail(i - 1)

            ln_stats("b", i, x2buf[i % 2], s1b, sqb, mb, vb, rb, nb, f"x2_{i}",
                     pre_sum=f"tat_{i}")

            wait("pool", f"nmrb_{i}", force=True)
            if i >= 1:
                wait("pool", f"tx2_{i-1}")
            emit("pool", lambda E, i=i: E.tensor_scalar(
                xln2[:], x2buf[i % 2][:], rb[:, i:i + 1], nb[:, i:i + 1],
                ALU.mult, ALU.add), f"ln2a_{i}")


        ffn_tx(T - 1)
        ffn_mm(T - 1)
        ffn_tail(T - 1)
        c_o = dcnt["o"]
        sched.append(("pool", lambda E, c_o=c_o: E.wait_ge(dsem["o"], c_o)))

        def runner(engname):
            def body(E):
                for en, thunk in sched:
                    if en == engname:
                        thunk(E)
            return body

        block.sync(runner("sync"))
        block.tensor(runner("pe"))
        block.vector(runner("dve"))
        block.scalar(runner("act"))
        block.gpsimd(runner("pool"))

    return nc


def kernel(L0_attn_x, L0_attn_w_q, L0_attn_b_q, L0_attn_w_k, L0_attn_b_k,
           L0_ffn_kernel, L0_ffn_bias, L0_ffn_a, L0_ffn_b):
    x = np.ascontiguousarray(np.asarray(L0_attn_x, dtype=np.float32))
    wq = np.ascontiguousarray(np.asarray(L0_attn_w_q, dtype=np.float32))
    bq = np.ascontiguousarray(np.asarray(L0_attn_b_q, dtype=np.float32))
    wk = np.ascontiguousarray(np.asarray(L0_attn_w_k, dtype=np.float32))
    bk = np.ascontiguousarray(np.asarray(L0_attn_b_k, dtype=np.float32))
    kern = np.ascontiguousarray(np.asarray(L0_ffn_kernel, dtype=np.float32))
    fb = np.ascontiguousarray(np.asarray(L0_ffn_bias, dtype=np.float32))
    a = np.ascontiguousarray(np.asarray(L0_ffn_a, dtype=np.float32))
    b = np.ascontiguousarray(np.asarray(L0_ffn_b, dtype=np.float32))
    S = x.shape[1]
    nc = build_nc(S)
    in_maps = [{"x": np.ascontiguousarray(x[c]), "wq": wq, "bq": bq, "wk": wk, "bk": bk,
                "kern": kern, "fb": fb, "a": a, "b": b} for c in range(x.shape[0])]
    res = run_bass_kernel_spmd(nc, in_maps, core_ids=list(range(x.shape[0])))
    return np.stack([r["out"] for r in res.results], axis=0)
